# version 11
# baseline (speedup 1.0000x reference)
import numpy as np
import ml_dtypes
import concourse.bass as bass
import concourse.mybir as mybir
from concourse.bass_utils import run_bass_kernel_spmd

BF = ml_dtypes.bfloat16
F32 = mybir.dt.float32
BF16 = mybir.dt.bfloat16
AF = mybir.ActivationFunctionType
ALU = mybir.AluOpType
AX = mybir.AxisListType

B, S, D, NH, HD = 4, 4096, 1024, 16, 64
TOK = 2048
NT_OWN = TOK // 128
NT_SEQ = S // 128
DCH = D // 128
NQB = S // 512
NKT = S // 128
F0, F1 = 4096, 2048
NE = 8
SG = 512
NEG_BIG = -1.0e30
EXP_SHIFT = -30.0
PAIR_GROUPS = [[0, 1], [2, 3], [4, 5], [6, 7]]



def gen_rope_tables():
    half = HD // 2
    inv_freq = 1.0 / (10000.0 ** (np.arange(half, dtype=np.float64) / half))
    ang = np.arange(S, dtype=np.float64)[:, None] * inv_freq[None, :]
    cos = np.cos(ang).astype(np.float32)
    sin = np.sin(ang).astype(np.float32)
    ct = np.empty((128, S), dtype=np.float32)
    st = np.empty((128, S), dtype=np.float32)
    for rr in range(128):
        j = rr % half
        ct[rr] = cos[:, j]
        st[rr] = sin[:, j] if (rr % HD) >= half else -sin[:, j]
    return ct, st


def l0_mask_bool():
    i = np.arange(S)[:, None]
    j = np.arange(S)[None, :]
    return (j <= i)


def l1_mask_bool():
    LOCAL = 256
    GIDX = np.array([0, 1, 2, 3])
    i = np.arange(S)[:, None]
    j = np.arange(S)[None, :]
    causal = j <= i
    local = causal & (j >= i - LOCAL + 1)
    g_row = np.isin(np.arange(S), GIDX)[:, None]
    g_col = np.isin(np.arange(S), GIDX)[None, :]
    return local | (g_row & causal) | (g_col & causal)


def gen_schedule(mask):
    patterns = []
    pat_key = {}
    sched = []
    for qb in range(NQB):
        row = []
        for kt in range(NKT):
            tile = mask[qb * 512:(qb + 1) * 512, kt * 128:(kt + 1) * 128].T
            if not tile.any():
                continue
            if tile.all():
                row.append((kt, None))
                continue
            key = tile.tobytes()
            if key not in pat_key:
                pat_key[key] = len(patterns)
                patterns.append(np.where(tile, 0.0, NEG_BIG).astype(np.float32))
            row.append((kt, pat_key[key]))
        sched.append(row)
    return np.stack(patterns), sched



class Chan:
    __slots__ = ("name", "step", "n", "sem", "producer")

    def __init__(self, name, step):
        self.name, self.step, self.n, self.sem = name, step, 0, None
        self.producer = None


class Prog:
    ENGINES = ("y", "g", "t", "v", "s")
    SECTION = {"y": "sync", "g": "gpsimd", "t": "tensor", "v": "vector", "s": "scalar"}

    def __init__(self):
        self.streams = {e: [] for e in self.ENGINES}
        self.chans = {}

    def chan(self, name, step=1):
        c = Chan(name, step)
        self.chans[name] = c
        return c

    def op(self, eng, fn, waits=(), inc=None, drain=None):
        ws = []
        for w in waits:
            if w is None:
                continue
            c, v = w
            assert v <= c.n, f"wait on future event {c.name}: {v} > {c.n}"
            if v > 0:
                ws.append((c, v))
        if inc is not None:
            if inc.producer is None:
                inc.producer = eng
            assert inc.producer == eng, f"chan {inc.name} inc from {eng} and {inc.producer}"
            inc.n += inc.step
        self.streams[eng].append((fn, tuple(ws), inc, drain, inc.n if inc else 0))
        return (inc, inc.n) if inc is not None else None

    def emit(self, nc):
        cross = {}
        for eng in self.ENGINES:
            for fn, ws, inc, drain, incv in self.streams[eng]:
                for c, v in ws:
                    if c.step == 1 and c.producer is not None and c.producer != eng:
                        cross.setdefault(c.name, set()).add(v)
        for c in self.chans.values():
            c.sem = nc.alloc_semaphore(c.name)
        with nc.Block() as block:
            for eng in self.ENGINES:
                stream = self.streams[eng]
                if not stream:
                    continue

                def make_body(stream=stream, eng=eng):
                    def body(engine):
                        waited = {}
                        for fn, ws, inc, drain, incv in stream:
                            for c, v in ws:
                                if waited.get(c.name, 0) < v:
                                    engine.wait_ge(c.sem, v)
                                    waited[c.name] = v
                            instr = fn(engine)
                            if inc is not None:
                                do_drain = drain
                                if do_drain is None:
                                    do_drain = (inc.step == 1 and eng in ("v", "s", "g")
                                                and incv in cross.get(inc.name, ()))
                                if do_drain:
                                    engine.drain().then_inc(inc.sem, inc.step)
                                else:
                                    instr.then_inc(inc.sem, inc.step)
                    return body

                getattr(block, self.SECTION[eng])(make_body())

    def counts(self):
        return {e: len(s) for e, s in self.streams.items()}



class Builder:
    def __init__(self, dbg=(), upto="all"):
        self.upto = upto
        self.dbg = set(dbg)
        self.nc = bass.Bass("TRN2", target_bir_lowering=False)
        self.P = Prog()
        ch = self.P.chan
        self.dS = ch("dS", 16)
        self.dY = ch("dY", 16)
        self.wS0 = ch("wS0", 16)
        self.wS1 = ch("wS1", 16)
        self.aw = ch("aw", 16)
        self.T = ch("T", 1)
        self.V = ch("V", 1)
        self.A = ch("A", 1)
        self.CC0 = ch("CC0", 1)
        self.CC1 = ch("CC1", 1)
        self.CC2 = ch("CC2", 1)
        self.dOUT = ch("dOUT", 16)

        self.mask0_pat, self.sched0 = gen_schedule(l0_mask_bool())
        self.mask1_pat, self.sched1 = gen_schedule(l1_mask_bool())
        assert self.mask0_pat.shape[0] <= 8, self.mask0_pat.shape
        assert self.mask1_pat.shape[0] <= 8, self.mask1_pat.shape

        self.declare_dram()
        self.alloc_sbuf()
        self.alloc_psum()

    def tev(self):
        return (self.T, self.T.n)

    def vev(self):
        return (self.V, self.V.n)

    def declare_dram(self):
        nc = self.nc
        DI = lambda n, s, d=F32: nc.dram_tensor(n, s, d, kind="ExternalInput")
        self.x_full = DI("x_full", [S, D])
        self.x_own = DI("x_own", [TOK, D])
        self.ident16_d = DI("ident16", [128, 128], BF16)
        self.ident32_d = DI("ident32", [128, 128], F32)
        self.ropec_d = DI("rope_cos", [128, S])
        self.ropes_d = DI("rope_sin", [128, S])
        self.m0_d = DI("l0_masks", list(self.mask0_pat.shape))
        self.m1_d = DI("l1_masks", list(self.mask1_pat.shape))
        self.onehot_d = DI("onehot", [8, 8 * 128])
        self.normw_d = DI("l1_normw_bcast", [128, D])

        self.l0_wq = DI("l0_wq_t", [D, 512])
        self.l0_wqs = DI("l0_wq_swap_t", [D, 512])
        self.l0_wk = DI("l0_wk_t", [D, 512])
        self.l0_wks = DI("l0_wk_swap_t", [D, 512])
        self.l0_wv = DI("l0_wv_t", [D, 512])
        self.l0_wo = DI("l0_wo_my", [512, D])
        self.l0_wg = DI("l0_wg_t", [D, 8])
        self.l0_w1 = DI("l0_w1_t", [NE, D, F0])
        self.l0_w3 = DI("l0_w3_t", [NE, D, F0])
        self.l0_w2 = DI("l0_w2_t", [NE, F0, D])

        self.l1_wq = DI("l1_wq_t", [D, 512])
        self.l1_wk = DI("l1_wk_t", [D, 512])
        self.l1_wv = DI("l1_wv_t", [D, 512])
        self.l1_wo = DI("l1_wo_my", [512, D])
        self.l1_rt = DI("l1_router_t", [D, 8])
        self.l1_w1 = DI("l1_w1", [NE, D, F1])
        self.l1_w3 = DI("l1_w3", [NE, D, F1])
        self.l1_w2 = DI("l1_w2", [NE, F1, D])

        DR = lambda n, s, d=F32: nc.dram_tensor(n, s, d, kind="Internal")
        self.cc_s0_in = [DR("cc_s0_inA", [S, 512], BF16), DR("cc_s0_inB", [S, 512], BF16)]
        self.cc_s0_out = [DR("cc_s0_outA", [TOK, 512], BF16), DR("cc_s0_outB", [TOK, 512], BF16)]
        self.cc_x1_in = DR("cc_x1_in", [TOK, D])
        self.cc_x1_out = DR("cc_x1_out", [S, D])
        self.cc_s1_in = [DR("cc_s1_inA", [S, 512], BF16), DR("cc_s1_inB", [S, 512], BF16)]
        self.cc_s1_out = [DR("cc_s1_outA", [TOK, 512], BF16), DR("cc_s1_outB", [TOK, 512], BF16)]

        self.out_own = nc.dram_tensor("out_own", [TOK, D], F32, kind="ExternalOutput")
        self.dbg_d = {}
        for name, shape, dt in (
            ("xn0", [128, DCH * S], BF16), ("acc0", [128, NT_OWN * D], F32),
            ("z", [128, NT_OWN * D], F32), ("g1", [128, NT_OWN * D], BF16),
            ("s0", [S, D], BF16), ("x1", [TOK, D], F32), ("s1", [S, D], BF16),
            ("xn1", [128, DCH * S], BF16), ("ctx0", [128, 4 * S], BF16),
            ("xnown", [128, DCH * TOK], BF16), ("xnlo", [128, DCH * TOK], BF16),
            ("cwall", [128, 128], F32), ("cwfm", [8, TOK], F32),
            ("cwb0", [128, TOK], BF16), ("h00", [128, 4 * 512], BF16),
            ("glog", [128, 128], F32), ("wghl", [128, 128], BF16),
        ):
            if name in self.dbg:
                self.dbg_d[name] = nc.dram_tensor("dbg_" + name, shape, dt, kind="ExternalOutput")

    def alloc_sbuf(self):
        nc = self.nc
        off = {}
        cur = 0

        def place(name, nbytes, align=32):
            nonlocal cur
            cur = (cur + align - 1) // align * align
            off[name] = cur
            cur += nbytes

        place("ident16", 256)
        place("ident32", 512)
        place("neg30", 32)
        place("eps5", 32)
        place("eps6", 32)
        place("st", 64)
        place("ones", 256)
        place("onehot", 4096)
        place("normw", 4096)
        place("gdbg", 512)
        place("gdbg16", 256)
        place("A", 65536)
        place("B", 32768)
        place("C", 32768)
        place("W", 57344)
        place("X", 12288)
        total = (cur + 31) // 32 * 32
        r = nc.bump_sbuf(total)
        assert r is not None, f"sbuf alloc failed: need {total}"
        base = r[0]

        def at(name, shape, dt, o):
            return nc.alloc_sbuf_tensor_at(name, shape, dt, offset=base + o)

        self.ident16 = at("s_ident16", [128, 128], BF16, off["ident16"])
        self.ident32 = at("s_ident32", [128, 128], F32, off["ident32"])
        self.neg30 = at("s_neg30", [128, 1], F32, off["neg30"])
        self.eps5 = at("s_eps5", [128, 1], F32, off["eps5"])
        self.eps6 = at("s_eps6", [128, 1], F32, off["eps6"])
        self.st = at("s_st", [128, 16], F32, off["st"])
        self.ones = at("s_ones", [1, 64], F32, off["ones"])
        self.onehot = at("s_onehot", [8, 8 * 128], F32, off["onehot"])
        self.normw = at("s_normw", [128, D], F32, off["normw"])

        A, Bo, C, W, X = off["A"], off["B"], off["C"], off["W"], off["X"]
        self.xn_att = at("s_xn_att", [128, DCH * S], BF16, A)
        self.acc = at("s_acc", [128, NT_OWN * D], F32, A)
        self.z = at("s_z", [128, NT_OWN * D], F32, A)
        self.qkv0 = dict(
            Q=at("s_q0Q", [128, S], BF16, Bo),
            K=at("s_q0K", [128, S], BF16, Bo + 8192),
            VA=at("s_q0VA", [128, NKT * 65], BF16, Bo + 16384),
            VB=at("s_q0VB", [128, NKT * 65], BF16, Bo + 20608),
            AT=at("s_q0AT", [128, 2048], BF16, Bo + 24832),
        )
        self.xn_own = at("s_xn_own", [128, DCH * TOK], BF16, Bo)
        self.zn_hi = at("s_zn_hi", [128, DCH * TOK], BF16, Bo)
        self.g1 = at("s_g1", [128, NT_OWN * D], BF16, Bo)
        self.ctx = at("s_ctx", [128, 4 * S], BF16, C)
        self.xn_lo = at("s_xn_lo", [128, DCH * TOK], BF16, C)
        self.cw_fm = at("s_cw_fm", [8, TOK], F32, W + 49152)
        self.cwb = at("s_cwb", [128, 512], BF16, X + 6144)
        self.ropeC = at("s_ropeC", [128, S], F32, W)
        self.ropeS = at("s_ropeS", [128, S], F32, W + 16384)
        self.masks = at("s_masks", [128, 8 * 512], F32, W + 32768)
        self.awq = at("s_awq", [128, DCH * 128], BF16, W + 49152)
        self.awqs = at("s_awqs", [128, DCH * 128], BF16, W + 51200)
        self.awk = at("s_awk", [128, DCH * 128], BF16, W + 53248)
        self.awks = at("s_awks", [128, DCH * 128], BF16, W + 55296)
        self.wo_sb = at("s_wo", [128, 4 * 1024], BF16, W)
        self.w1_sb = [at(f"s_w1_{i}", [128, DCH * SG], BF16, W + i * 24576) for i in range(2)]
        self.w3_sb = [at(f"s_w3_{i}", [128, DCH * SG], BF16, W + i * 24576 + 8192) for i in range(2)]
        self.w2_sb = [at(f"s_w2_{i}", [128, 4 * 1024], BF16, W + i * 24576 + 16384) for i in range(2)]
        self.qkv1 = dict(
            Q=at("s_q1Q", [128, S], BF16, W),
            K=at("s_q1K", [128, S], BF16, W + 8192),
            VA=at("s_q1VA", [128, NKT * 65], BF16, W + 16384),
            VB=at("s_q1VB", [128, NKT * 65], BF16, W + 20736),
            AT=at("s_q1AT", [128, 2048], BF16, W + 25024),
        )
        self.xs = at("s_xs", [128, D], F32, X)
        self.xq = at("s_xq", [128, D], F32, X + 4096)
        self.xntm = at("s_xntm", [128, D], BF16, X + 8192)
        self.xlotm = at("s_xlotm", [128, D], BF16, X + 10240)
        self.rope_t1 = at("s_ropet1", [128, 512], F32, X)
        self.rope_t2 = at("s_ropet2", [128, 512], F32, X + 2048)
        self.awv0 = at("s_awv0", [128, DCH * 128], BF16, Bo + 28992)
        self.sumrow = at("s_sumrow", [1, 512], F32, X + 6144)
        self.sumrec = at("s_sumrec", [1, 512], F32, X + 6400)
        self.bcast = at("s_bcast", [64, 512], F32, X + 8192)
        self.s_out0 = at("s_sout0", [128, 2 * 512], BF16, W + 8192)
        self.logits = at("s_logits", [128, 8], F32, X + 8192)
        self.gtmp = at("s_gtmp", [128, 8], F32, X + 8256)
        self.gsel = at("s_gsel", [128, 8], F32, X + 8320)
        self.gmask = at("s_gmask", [128, 8], F32, X + 8384)
        self.wg32 = at("s_wg32", [128, 64], F32, X + 8448)
        self.wg_hi = at("s_wg_hi", [128, 64], BF16, X + 8704)
        self.wg_lo = at("s_wg_lo", [128, 64], BF16, X + 8832)
        self.cw_all = at("s_cw_all", [128, 128], F32, X + 8960)
        self.ab_a = [at("s_ab_a0", [128, 512], BF16, X),
                     at("s_ab_a1", [128, 512], BF16, X + 10240)]
        self.ab_b = [at("s_ab_b0", [128, 512], BF16, X + 1024),
                     at("s_ab_b1", [128, 512], BF16, X + 11264)]
        self.h_sb = at("s_h_sb", [128, 4 * 512], BF16, X + 2048)
        self.comb_t = at("s_comb_t", [128, D], F32, X + 4096)
        self.comb_s = at("s_comb_s", [128, D], BF16, X + 8192)
        self.comb_g = at("s_comb_g", [128, D], BF16, X + 10240)
        self.wo1_sb = at("s_wo1", [128, 4 * 1024], BF16, X)
        self.s_out1 = at("s_sout1", [128, 2 * 512], BF16, X + 8192)
        self.gdbg = at("s_gdbg", [128, 128], F32, off["gdbg"])
        self.gdbg16 = at("s_gdbg16", [128, 128], BF16, off["gdbg16"])

    def alloc_psum(self):
        nc = self.nc
        self.pb0 = nc.alloc_psum_tensor("pb0", [128, 1024], F32)
        self.pb1 = nc.alloc_psum_tensor("pb1", [128, 1024], F32)
        self.pacc = nc.alloc_psum_tensor("pacc", [128, 1024], F32)
        self.pt16 = [nc.alloc_psum_tensor("pt16_0", [128, 1024], BF16)]
        self.paux7 = nc.alloc_psum_tensor("paux7", [128, 512], F32)


    def load_constants(self):
        o = self.P.op
        dY, V = self.dY, self.V
        for dst, src in ((self.ident16, self.ident16_d), (self.ident32, self.ident32_d),
                         (self.onehot, self.onehot_d), (self.normw, self.normw_d),
                         (self.ropeC, self.ropec_d), (self.ropeS, self.ropes_d)):
            o("y", lambda e, dst=dst, src=src: e.dma_start(out=dst[:], in_=src[:]), inc=dY)
        np0 = self.mask0_pat.shape[0]
        o("y", lambda e, np0=np0: e.dma_start(
            out=self.masks[:, 0:np0 * 512], in_=self.m0_d.rearrange("n p q -> p n q")), inc=dY)
        for t, v in ((self.neg30, EXP_SHIFT), (self.eps5, 1e-5), (self.eps6, 1e-6)):
            o("v", lambda e, t=t, v=v: e.memset(t[:], v), inc=V)
        o("v", lambda e: e.memset(self.ones[:], 1.0), inc=V)
        self.ev_const = (dY, dY.n)
        self.ev_constv = (V, V.n)

    def ln_tile(self, ev_x, eps, want_lo, rms):
        o = self.P.op
        st, xs, xq = self.st, self.xs, self.xq
        ev_sq = o("s", lambda e: e.activation(xq[:], xs[:], AF.Square,
                                              accum_out=st[:, 1:2]),
                  waits=[ev_x], inc=self.A)
        if not rms:
            ev_cp = o("s", lambda e: e.activation(self.xntm[:], xs[:], AF.Copy,
                                                  accum_out=st[:, 0:1]),
                      waits=[ev_x], inc=self.A)
            ev = o("v", lambda e: e.tensor_scalar(st[:, 2:3], st[:, 0:1], 1.0 / D, None,
                                                  op0=ALU.mult), waits=[ev_cp], inc=self.V)
            ev = o("v", lambda e: e.tensor_scalar(st[:, 3:4], st[:, 2:3], st[:, 2:3], None,
                                                  op0=ALU.mult), waits=[ev], inc=self.V)
            ev_var = o("v", lambda e: e.scalar_tensor_tensor(
                st[:, 4:5], st[:, 1:2], 1.0 / D, st[:, 3:4], op0=ALU.mult, op1=ALU.subtract),
                waits=[ev, ev_sq], inc=self.V)
        else:
            ev_var = o("v", lambda e: e.tensor_scalar(st[:, 4:5], st[:, 1:2], 1.0 / D, None,
                                                      op0=ALU.mult), waits=[ev_sq], inc=self.V)
        ev_sd = o("s", lambda e, eps=eps: e.activation(st[:, 5:6], st[:, 4:5], AF.Sqrt,
                                                       bias=eps[:]), waits=[ev_var], inc=self.A)
        ev_r = o("v", lambda e: e.reciprocal(st[:, 6:7], st[:, 5:6]), waits=[ev_sd], inc=self.V)
        if not rms:
            ev_nmr = o("v", lambda e: e.tensor_scalar(
                st[:, 7:8], st[:, 2:3], st[:, 6:7], -1.0, op0=ALU.mult, op1=ALU.mult),
                waits=[ev_r], inc=self.V)
            ev_hi = o("v", lambda e: e.tensor_scalar(
                self.xntm[:], xs[:], st[:, 6:7], st[:, 7:8], op0=ALU.mult, op1=ALU.add),
                waits=[ev_nmr], inc=self.V)
            if want_lo:
                ev_f = o("v", lambda e: e.tensor_scalar(
                    xq[:], xs[:], st[:, 6:7], st[:, 7:8], op0=ALU.mult, op1=ALU.add),
                    waits=[ev_nmr], inc=self.V)
                ev_lo = o("v", lambda e: e.scalar_tensor_tensor(
                    self.xlotm[:], self.xntm[:], -1.0, xq[:], op0=ALU.mult, op1=ALU.add),
                    waits=[ev_hi, ev_f], inc=self.V)
        else:
            ev_hi = o("v", lambda e: e.scalar_tensor_tensor(
                self.xntm[:], xs[:], st[:, 6:7], self.normw[:], op0=ALU.mult, op1=ALU.mult),
                waits=[ev_r, self.ev_const], inc=self.V)
            if want_lo:
                ev_f = o("v", lambda e: e.scalar_tensor_tensor(
                    xq[:], xs[:], st[:, 6:7], self.normw[:], op0=ALU.mult, op1=ALU.mult),
                    waits=[ev_r], inc=self.V)
                ev_lo = o("v", lambda e: e.scalar_tensor_tensor(
                    self.xlotm[:], self.xntm[:], -1.0, xq[:], op0=ALU.mult, op1=ALU.add),
                    waits=[ev_hi, ev_f], inc=self.V)
        out = {"hi": ev_hi}
        if want_lo:
            out["lo"] = ev_lo
            out["xs_free"] = ev_lo
        else:
            out["xs_free"] = ev_hi
        return out

    def transpose_tile(self, ev_src, src, dst, dst_col, bank, ev_bank_free):
        o = self.P.op
        pt = self.pt16[0]
        last = None
        for ch in range(DCH):
            w = []
            if ch == 0:
                w = [ev_src, self.ev_const]
                if ev_bank_free is not None:
                    w.append(ev_bank_free)
            last = o("t", lambda e, ch=ch, pt=pt, src=src: e.transpose(
                pt[:, ch * 128:(ch + 1) * 128], src[:, ch * 128:(ch + 1) * 128],
                self.ident16[:]), waits=w, inc=self.T)
        dview = dst.rearrange("p (c n) -> p c n", c=DCH)[:, :, dst_col:dst_col + 128]
        sview = pt.rearrange("p (c n) -> p c n", c=DCH)
        ev_ev = o("v", lambda e, dview=dview, sview=sview: e.tensor_copy(dview, sview),
                  waits=[last], inc=self.V)
        return ev_ev

    def ln_transpose_full(self, src_dram, ntiles, dst, want_lo, lo_dst, rms, eps, guard=()):
        o = self.P.op
        ev_ev = None
        ev_ev2 = None
        ev_xs_free = None
        self.ln_tile_evs = []
        for tt in range(ntiles):
            w = list(guard) + [self.ev_const, self.ev_constv]
            if ev_xs_free is not None:
                w.append(ev_xs_free)
            ev_x = o("s", lambda e, tt=tt, src_dram=src_dram: e.dma_start(
                out=self.xs[:], in_=src_dram[tt * 128:(tt + 1) * 128, :]),
                waits=w, inc=self.dS)
            evs = self.ln_tile(ev_x, eps, want_lo, rms)
            ev_ev = self.transpose_tile(evs["hi"], self.xntm, dst, tt * 128, 0, ev_ev)
            self.ln_tile_evs.append(ev_ev)
            if want_lo:
                ev_ev2 = self.transpose_tile(evs["lo"], self.xlotm, lo_dst, tt * 128, 0, ev_ev)
                ev_ev = ev_ev2
                ev_xs_free = (self.V, max(ev_ev2[1], evs["xs_free"][1]))
            else:
                ev_xs_free = (self.V, max(ev_ev[1], evs["xs_free"][1]))
        return ev_xs_free

    def ln_full_f32(self, src_dram, dst, guard=(), per_tile_guards=None):
        o = self.P.op
        ev_xs_free = None
        self.ln_z_tile_evs = []
        for tt in range(NT_OWN):
            w = list(guard) + [self.ev_constv]
            if per_tile_guards is not None:
                w.append(per_tile_guards[tt])
            if ev_xs_free is not None:
                w.append(ev_xs_free)
            ev_x = o("s", lambda e, tt=tt, src_dram=src_dram: e.dma_start(
                out=self.xs[:], in_=src_dram[tt * 128:(tt + 1) * 128, :]),
                waits=w, inc=self.dS)
            evs = self.ln_tile(ev_x, self.eps5, False, False)
            ev_z = o("v", lambda e, tt=tt, dst=dst: e.tensor_scalar(
                dst[:, tt * 1024:(tt + 1) * 1024], self.xs[:],
                self.st[:, 6:7], self.st[:, 7:8], op0=ALU.mult, op1=ALU.add),
                waits=[evs["hi"]], inc=self.V)
            ev_xs_free = ev_z
            self.ln_z_tile_evs.append(ev_z)
        return ev_xs_free

    def attention_v2(self, layer, ev_xn, ev_w_guard, ev_xn_tiles=None):
        if ev_xn_tiles is None:
            ev_xn_tiles = [ev_xn] * NT_SEQ
        o = self.P.op
        if layer == 0:
            wq_d, wqs_d, wk_d, wks_d, wv_d, wo_d = (self.l0_wq, self.l0_wqs, self.l0_wk,
                                                    self.l0_wks, self.l0_wv, self.l0_wo)
            sched, cc_in, cc_out = self.sched0, self.cc_s0_in, self.cc_s0_out
            rope = True
            bufs = self.qkv0
            wv_sb = self.awv0
            wo_sb, s_out = self.wo_sb, self.s_out0
            cchan = self.CC0
            dbg_name = "s0"
        else:
            wq_d, wqs_d, wk_d, wks_d, wv_d, wo_d = (self.l1_wq, None, self.l1_wk,
                                                    None, self.l1_wv, self.l1_wo)
            sched, cc_in, cc_out = self.sched1, self.cc_s1_in, self.cc_s1_out
            rope = False
            bufs = self.qkv1
            wv_sb = self.awqs
            wo_sb, s_out = self.wo1_sb, self.s_out1
            cchan = self.CC2
            dbg_name = "s1"
        QP, KP, AT_ = bufs["Q"], bufs["K"], bufs["AT"]
        VS = (bufs["VA"], bufs["VB"])
        xnv = self.xn_att.rearrange("p (c n) -> p c n", c=DCH)

        ev_vones = {}
        for s_, vt in ((0, VS[0]), (1, VS[1])):
            ev_vones[s_] = o("v", lambda e, vt=vt: e.memset(
                vt.rearrange("p (k c) -> p k c", c=65)[:, :, 64:65], 1.0),
                waits=[ev_w_guard], inc=self.V)

        self._aw_free = ev_w_guard
        self._pb1_free = ev_w_guard
        gstate = {}

        def proj_gen(g):
            slot = g & 1
            b0 = 64 * slot
            wts = [(self.awq, wq_d), (self.awk, wk_d), (wv_sb, wv_d)]
            if rope:
                wts += [(self.awqs, wqs_d), (self.awks, wks_d)]
            first = True
            for sb, dsrc in wts:
                w = [self._aw_free] if first else []
                o("g", lambda e, sb=sb, dsrc=dsrc, g=g: e.dma_start(
                    out=sb.rearrange("p (kk c) -> p kk c", c=64)[:, 0:DCH, :],
                    in_=dsrc.rearrange("(kk p) (gg c) -> gg p kk c", p=128, c=64)[g]),
                    waits=w, inc=self.aw)
                first = False
            ev_aw = (self.aw, self.aw.n)
            ev_bank2 = None
            ev_bank3 = None
            ev_wr = {}
            for dst, srcw, srcws in ((QP, self.awq, self.awqs), (KP, self.awk, self.awks)):
                ev_written = None
                for tb in range(NQB):
                    wmm = [ev_aw, ev_xn_tiles[tb * 4 + 3]]
                    if ev_bank2 is not None:
                        wmm.append(ev_bank2)
                    else:
                        wmm.append(self._pb1_free)
                    for kk in range(DCH):
                        o("t", lambda e, kk=kk, tb=tb, srcw=srcw, xnv=xnv: e.matmul(
                            self.pb1[0:64, 0:512], srcw[:, kk * 64:(kk + 1) * 64],
                            xnv[:, kk, tb * 512:(tb + 1) * 512],
                            start=(kk == 0), stop=(kk == DCH - 1)),
                            waits=wmm if kk == 0 else [], inc=self.T)
                    yield
                    if rope:
                        wmm2 = [ev_bank3] if ev_bank3 is not None else []
                        for kk in range(DCH):
                            o("t", lambda e, kk=kk, tb=tb, srcws=srcws, xnv=xnv: e.matmul(
                                self.pb1[0:64, 512:1024], srcws[:, kk * 64:(kk + 1) * 64],
                                xnv[:, kk, tb * 512:(tb + 1) * 512],
                                start=(kk == 0), stop=(kk == DCH - 1)),
                                waits=wmm2 if kk == 0 else [], inc=self.T)
                        ev_mm = self.tev()
                        ev1 = o("v", lambda e, tb=tb: e.tensor_mul(
                            self.rope_t1[0:64, :], self.pb1[0:64, 512:1024],
                            self.ropeS[0:64, tb * 512:(tb + 1) * 512]),
                            waits=[ev_mm, self.ev_const], inc=self.V)
                        ev_bank3 = ev1
                        ev2 = o("v", lambda e, tb=tb: e.tensor_mul(
                            self.rope_t2[0:64, :], self.pb1[0:64, 0:512],
                            self.ropeC[0:64, tb * 512:(tb + 1) * 512]), inc=self.V)
                        ev_bank2 = ev2
                        ev_written = o("v", lambda e, dst=dst, tb=tb, b0=b0: e.tensor_add(
                            dst[b0:b0 + 64, tb * 512:(tb + 1) * 512],
                            self.rope_t1[0:64, :], self.rope_t2[0:64, :]),
                            waits=[ev2], inc=self.V)
                        yield
                    else:
                        ev_mm = self.tev()
                        ev_written = o("v", lambda e, dst=dst, tb=tb, b0=b0: e.tensor_copy(
                            dst[b0:b0 + 64, tb * 512:(tb + 1) * 512], self.pb1[0:64, 0:512]),
                            waits=[ev_mm], inc=self.V)
                        ev_bank2 = ev_written
                ev_wr[id(dst)] = ev_written
            vtile = VS[slot]
            vv = vtile.rearrange("p (k c) -> p k c", c=65)
            vps_free = {0: ev_bank2, 1: ev_bank3 if ev_bank3 is not None else ev_bank2}
            ev_vev = None
            for kt in range(NKT):
                pv = self.pb1[:, (kt % 2) * 512:(kt % 2) * 512 + 64]
                wmm = [ev_aw, ev_xn_tiles[kt]]
                if vps_free[kt % 2] is not None:
                    wmm.append(vps_free[kt % 2])
                for kk in range(DCH):
                    o("t", lambda e, kk=kk, kt=kt, pv=pv, wv_sb=wv_sb, xnv=xnv: e.matmul(
                        pv, xnv[:, kk, kt * 128:(kt + 1) * 128],
                        wv_sb[:, kk * 64:(kk + 1) * 64],
                        start=(kk == 0), stop=(kk == DCH - 1)),
                        waits=wmm if kk == 0 else [], inc=self.T)
                ev_mm = self.tev()
                ev_vev = o("v", lambda e, kt=kt, pv=pv, vv=vv: e.tensor_copy(
                    vv[:, kt, 0:64], pv), waits=[ev_mm, ev_vones[slot]], inc=self.V)
                vps_free[kt % 2] = ev_vev
                yield
            self._aw_free = self.tev()
            self._pb1_free = ev_vev
            gstate[g] = ((self.V, max(ev_wr[id(QP)][1], ev_wr[id(KP)][1])), ev_vev)

        gen_done = {}

        def pump(gen, n, gi=None):
            if gen is None:
                return True
            for _ in range(n):
                try:
                    next(gen)
                except StopIteration:
                    if gi is not None:
                        gen_done[gi] = True
                    return True
            return False

        NHL = 8
        gens = {g: proj_gen(g) for g in range(NHL)}
        pump(gens[0], 10 ** 6)
        ev_ctx_last = None
        ppp = 1 if layer == 0 else 2
        for g in range(NHL):
            nxt = gens.get(g + 1)
            gid = g + 1
            gen_done[gid] = nxt is None
            b0 = 64 * (g & 1)
            vtile = VS[g & 1]
            ev_qk, ev_v = gstate[g]
            pump(nxt, 4, gid)
            for qb in range(NQB):
                entries = sched[qb]
                nent = len(entries)
                slot_free = None
                at_free = {0: None, 1: None}
                ev_av_last = None
                ev_prev_ctx = ev_ctx_last
                npairs = (nent + 1) // 2
                for pi in range(npairs):
                    pair = entries[pi * 2: pi * 2 + 2]
                    at_sl = AT_[:, (pi % 2) * 1024:(pi % 2 + 1) * 1024]
                    two_slots = gen_done.get(gid, True)
                    pst = self.pb1 if (two_slots and pi % 2 == 1) else self.pb0
                    wmm = [ev_qk, ev_v, self.ev_constv]
                    sf = slot_free if not isinstance(slot_free, dict) else None
                    if two_slots:
                        if not isinstance(slot_free, dict):
                            slot_free = {0: slot_free, 1: self._pb1_free}
                        sf = slot_free[pi % 2]
                    if sf is not None:
                        wmm.append(sf)
                    ev_masks = []
                    for j, (kt, pat) in enumerate(pair):
                        o("t", lambda e, kt=kt, j=j, qb=qb, b0=b0, pst=pst: e.matmul(
                            pst[:, j * 512:(j + 1) * 512],
                            KP[b0:b0 + 64, kt * 128:(kt + 1) * 128],
                            QP[b0:b0 + 64, qb * 512:(qb + 1) * 512],
                            start=True, stop=True),
                            waits=wmm if j == 0 else [], inc=self.T)
                        if pat is not None:
                            ev_mm = self.tev()
                            ev_mk = o("v", lambda e, j=j, pat=pat, pst=pst: e.tensor_add(
                                pst[:, j * 512:(j + 1) * 512],
                                pst[:, j * 512:(j + 1) * 512],
                                self.masks[:, pat * 512:(pat + 1) * 512]),
                                waits=[ev_mm, self.ev_const], inc=self.V)
                            ev_masks.append(ev_mk)
                    ev_mm_pair = self.tev()
                    width = 512 * len(pair)
                    wexp = [ev_mm_pair, self.ev_constv] + ev_masks
                    if at_free[pi % 2] is not None:
                        wexp.append(at_free[pi % 2])
                    ev_e = o("s", lambda e, at_sl=at_sl, width=width, pst=pst: e.activation(
                        at_sl[:, 0:width], pst[:, 0:width], AF.Exp,
                        bias=self.neg30[:], scale=0.125), waits=wexp, inc=self.A)
                    if isinstance(slot_free, dict):
                        slot_free[pi % 2] = ev_e
                        self._pb1_free = ev_e if pi % 2 == 1 else self._pb1_free
                    else:
                        slot_free = ev_e
                    pump(nxt, ppp, gid)
                    for j, (kt, pat) in enumerate(pair):
                        first = (pi == 0 and j == 0)
                        last = (pi * 2 + j == nent - 1)
                        wav = [ev_e]
                        if first and ev_prev_ctx is not None:
                            wav.append(ev_prev_ctx)
                        o("t", lambda e, kt=kt, j=j, at_sl=at_sl, vtile=vtile,
                            first=first, last=last: e.matmul(
                            self.pacc[0:65, 0:512],
                            vtile[:, kt * 65: kt * 65 + 65],
                            at_sl[:, j * 512:(j + 1) * 512],
                            start=first, stop=last),
                            waits=wav if j == 0 else [], inc=self.T)
                    at_free[pi % 2] = self.tev()
                    ev_av_last = self.tev()
                ev_sr = o("v", lambda e: e.tensor_copy(self.sumrow[:], self.pacc[64:65, 0:512]),
                          waits=[ev_av_last], inc=self.V)
                ev_rc = o("v", lambda e: e.reciprocal(self.sumrec[:], self.sumrow[:]),
                          waits=[ev_sr], inc=self.V)
                ev_bc = o("t", lambda e: e.matmul(
                    self.pacc[0:64, 512:1024], self.ones[:], self.sumrec[:],
                    start=True, stop=True), waits=[ev_rc, self.ev_constv], inc=self.T)
                ev_bs = o("v", lambda e: e.tensor_copy(self.bcast[:], self.pacc[0:64, 512:1024]),
                          waits=[ev_bc], inc=self.V)
                ev_cx = o("v", lambda e, g=g, b0=b0, qb=qb: e.tensor_mul(
                    self.ctx[b0:b0 + 64, (g // 2) * S + qb * 512: (g // 2) * S + (qb + 1) * 512],
                    self.pacc[0:64, 0:512], self.bcast[:]),
                    waits=[ev_bs], inc=self.V)
                ev_ctx_last = ev_cx
            pump(nxt, 10 ** 6)
        self.ev_attn_done = ev_ctx_last
        if layer == 0 and "ctx0" in self.dbg_d:
            o("y", lambda e: e.dma_start(out=self.dbg_d["ctx0"][:], in_=self.ctx[:]),
              waits=[ev_ctx_last], inc=self.dY)

        ev_wo = o("g", lambda e, wo_sb=wo_sb, wo_d=wo_d: e.dma_start(
            out=wo_sb[:], in_=wo_d.rearrange("(cc p) o -> p cc o", p=128)),
            waits=[self.ev_attn_done], inc=self.aw)
        ctxv = self.ctx.rearrange("p (c n) -> p c n", c=4)
        ev_store_last = None
        ps_free = {0: None, 1: None}
        sl_free = {0: None, 1: None}
        for oh in range(2):
            for tt in range(NT_SEQ):
                sl = (tt * 2 + oh) % 2
                ps = self.pacc[:, sl * 512:(sl + 1) * 512]
                wmm = [ev_wo, self.ev_attn_done]
                if ps_free[sl] is not None:
                    wmm.append(ps_free[sl])
                for cc in range(4):
                    o("t", lambda e, cc=cc, tt=tt, oh=oh, ps=ps, ctxv=ctxv, wo_sb=wo_sb: e.matmul(
                        ps, ctxv[:, cc, tt * 128:(tt + 1) * 128],
                        wo_sb[:, cc * 1024 + oh * 512: cc * 1024 + (oh + 1) * 512],
                        start=(cc == 0), stop=(cc == 3)),
                        waits=wmm if cc == 0 else [], inc=self.T)
                ev_mm = self.tev()
                wv = [ev_mm]
                if sl_free[sl] is not None:
                    wv.append(sl_free[sl])
                ev_ev = o("v", lambda e, sl=sl, ps=ps, s_out=s_out: e.tensor_copy(
                    s_out[:, sl * 512:(sl + 1) * 512], ps), waits=wv, inc=self.V)
                ps_free[sl] = ev_ev
                ev_st = o("y", lambda e, tt=tt, oh=oh, sl=sl, cc_in=cc_in, s_out=s_out: e.dma_start(
                    out=cc_in[oh][tt * 128:(tt + 1) * 128, :],
                    in_=s_out[:, sl * 512:(sl + 1) * 512]), waits=[ev_ev], inc=self.dY)
                sl_free[sl] = ev_st
                ev_store_last = ev_st
            o("g", lambda e, oh=oh, cc_in=cc_in, cc_out=cc_out: e.collective_compute(
                "ReduceScatter", ALU.add, replica_groups=PAIR_GROUPS,
                ins=[cc_in[oh][:]], outs=[cc_out[oh][:]]),
                waits=[ev_store_last], inc=cchan, drain=False)
        if dbg_name in self.dbg_d:
            for oh in range(2):
                o("y", lambda e, oh=oh, cc_in=cc_in, dbg_name=dbg_name: e.dma_start(
                    out=self.dbg_d[dbg_name][:, oh * 512:(oh + 1) * 512].rearrange(
                        "a b -> a b") if False else
                    self.dbg_d[dbg_name].rearrange("a (h c) -> a h c", h=2)[:, oh, :],
                    in_=cc_in[oh][:]), waits=[ev_store_last], inc=self.dY)
        self.ev_cc = (cchan, cchan.n)

    def attention(self, layer, ev_xn, ev_w_guard, ev_xn_tiles=None):
        if ev_xn_tiles is None:
            ev_xn_tiles = [ev_xn] * NT_SEQ
        o = self.P.op
        if layer == 0:
            wq_d, wqs_d, wk_d, wks_d, wv_d, wo_d = (self.l0_wq, self.l0_wqs, self.l0_wk,
                                                    self.l0_wks, self.l0_wv, self.l0_wo)
            sched, cc_in, cc_out = self.sched0, self.cc_s0_in, self.cc_s0_out
            rope = True
            bufs = self.qkv0
            wv_sb = self.awv0
            wo_sb, s_out = self.wo_sb, self.s_out0
            cchan = self.CC0
            dbg_name = "s0"
        else:
            wq_d, wqs_d, wk_d, wks_d, wv_d, wo_d = (self.l1_wq, None, self.l1_wk,
                                                    None, self.l1_wv, self.l1_wo)
            sched, cc_in, cc_out = self.sched1, self.cc_s1_in, self.cc_s1_out
            rope = False
            bufs = self.qkv1
            wv_sb = self.awk
            wo_sb, s_out = self.wo1_sb, self.s_out1
            cchan = self.CC2
            dbg_name = "s1"
        QB_, KB_, VA_, VB_, AT_ = bufs["Q"], bufs["K"], bufs["VA"], bufs["VB"], bufs["AT"]
        if layer == 1:
            wv_sb = self.awqs
        xnv = self.xn_att.rearrange("p (c n) -> p c n", c=DCH)

        ev_vones = {}
        for nm, vt in (("A", VA_), ("B", VB_)):
            ev_vones[nm] = o("v", lambda e, vt=vt: e.memset(
                vt.rearrange("p (k c) -> p k c", c=65)[:, :, 64:65], 1.0),
                waits=[ev_w_guard], inc=self.V)

        ev_ctx_last = None
        ev_aw_free = ev_w_guard
        ev_group_done = None
        vps_free = {0: None, 1: None}
        for g in range(4):
            wts = [(self.awq, wq_d), (self.awk, wk_d), (wv_sb, wv_d)]
            if rope:
                wts += [(self.awqs, wqs_d), (self.awks, wks_d)]
            first = True
            for sb, dsrc in wts:
                w = [ev_aw_free] if first else []
                o("g", lambda e, sb=sb, dsrc=dsrc, g=g: e.dma_start(
                    out=sb[:],
                    in_=dsrc.rearrange("(kk p) (gg c) -> gg p kk c", p=128, c=128)[g]),
                    waits=w, inc=self.aw)
                first = False
            ev_aw = (self.aw, self.aw.n)

            ev_dst_done = {}
            for dst, srcw, srcws, pband in ((QB_, self.awq, self.awqs, self.pb0),
                                            (KB_, self.awk, self.awks, self.pb1)):
                ev_written = None
                for tb in range(NQB):
                    wmm = [ev_aw, ev_xn_tiles[tb * 4 + 3]]
                    if ev_group_done is not None:
                        wmm.append(ev_group_done)
                    if ev_written is not None:
                        wmm.append(ev_written)
                    for kk in range(DCH):
                        o("t", lambda e, kk=kk, tb=tb, srcw=srcw, pband=pband, xnv=xnv: e.matmul(
                            pband[:, 0:512], srcw[:, kk * 128:(kk + 1) * 128],
                            xnv[:, kk, tb * 512:(tb + 1) * 512],
                            start=(kk == 0), stop=(kk == DCH - 1)),
                            waits=wmm if kk == 0 else [], inc=self.T)
                    if rope:
                        for kk in range(DCH):
                            o("t", lambda e, kk=kk, tb=tb, srcws=srcws, pband=pband, xnv=xnv: e.matmul(
                                pband[:, 512:1024], srcws[:, kk * 128:(kk + 1) * 128],
                                xnv[:, kk, tb * 512:(tb + 1) * 512],
                                start=(kk == 0), stop=(kk == DCH - 1)), inc=self.T)
                        ev_mm = self.tev()
                        ev1 = o("v", lambda e, tb=tb, pband=pband: e.tensor_mul(
                            self.rope_t1[:], pband[:, 512:1024],
                            self.ropeS[:, tb * 512:(tb + 1) * 512]),
                            waits=[ev_mm, self.ev_const], inc=self.V)
                        ev2 = o("v", lambda e, tb=tb, pband=pband: e.tensor_mul(
                            self.rope_t2[:], pband[:, 0:512],
                            self.ropeC[:, tb * 512:(tb + 1) * 512]), inc=self.V)
                        ev_written = o("v", lambda e, dst=dst, tb=tb: e.tensor_add(
                            dst[:, tb * 512:(tb + 1) * 512], self.rope_t1[:], self.rope_t2[:]),
                            waits=[ev2], inc=self.V)
                    else:
                        ev_mm = self.tev()
                        ev_written = o("v", lambda e, dst=dst, tb=tb, pband=pband: e.tensor_copy(
                            dst[:, tb * 512:(tb + 1) * 512], pband[:, 0:512]),
                            waits=[ev_mm], inc=self.V)
                ev_dst_done[id(dst)] = ev_written
            ev_qk_done = (self.V, max(ev_dst_done[id(QB_)][1], ev_dst_done[id(KB_)][1]))

            ev_vev = None
            for kt in range(NKT):
                wmm = [ev_aw, ev_xn_tiles[kt]]
                if ev_group_done is not None:
                    wmm.append(ev_group_done)
                if vps_free[kt % 2] is not None:
                    wmm.append(vps_free[kt % 2])
                pv = self.pacc[:, (kt % 2) * 512:(kt % 2) * 512 + 128]
                for kk in range(DCH):
                    o("t", lambda e, kk=kk, kt=kt, pv=pv, wv_sb=wv_sb, xnv=xnv: e.matmul(
                        pv, xnv[:, kk, kt * 128:(kt + 1) * 128],
                        wv_sb[:, kk * 128:(kk + 1) * 128],
                        start=(kk == 0), stop=(kk == DCH - 1)),
                        waits=wmm if kk == 0 else [], inc=self.T)
                ev_mm = self.tev()
                o("v", lambda e, kt=kt, pv=pv, VA_=VA_: e.tensor_copy(
                    VA_.rearrange("p (k c) -> p k c", c=65)[:, kt, 0:64], pv[:, 0:64]),
                    waits=[ev_mm, ev_vones["A"]], inc=self.V)
                ev_vev = o("v", lambda e, kt=kt, pv=pv, VB_=VB_: e.tensor_copy(
                    VB_.rearrange("p (k c) -> p k c", c=65)[:, kt, 0:64], pv[:, 64:128]),
                    waits=[ev_vones["B"]], inc=self.V)
                vps_free[kt % 2] = ev_vev
            ev_v_done = ev_vev
            ev_aw_free = self.tev()

            for hloc in range(2):
                hb = 64 * hloc
                vtile = VA_ if hloc == 0 else VB_
                for qb in range(NQB):
                    entries = sched[qb]
                    nent = len(entries)
                    slot_free = {0: None, 1: None}
                    at_free = {0: None, 1: None}
                    ev_av_last = None
                    ev_prev_ctx = ev_ctx_last
                    npairs = (nent + 1) // 2
                    for pi in range(npairs):
                        pair = entries[pi * 2: pi * 2 + 2]
                        slot = pi % 2
                        pst = self.pb0 if slot == 0 else self.pb1
                        at_sl = AT_[:, slot * 1024:(slot + 1) * 1024]
                        wmm = [ev_qk_done, ev_v_done, self.ev_constv]
                        if slot_free[slot] is not None:
                            wmm.append(slot_free[slot])
                        ev_masks = []
                        for j, (kt, pat) in enumerate(pair):
                            o("t", lambda e, kt=kt, j=j, qb=qb, pst=pst, hb=hb, QB_=QB_, KB_=KB_: e.matmul(
                                pst[:, j * 512:(j + 1) * 512],
                                KB_[hb:hb + 64, kt * 128:(kt + 1) * 128],
                                QB_[hb:hb + 64, qb * 512:(qb + 1) * 512],
                                start=True, stop=True),
                                waits=wmm if j == 0 else [], inc=self.T)
                            if pat is not None:
                                ev_mm = self.tev()
                                ev_mk = o("v", lambda e, j=j, pat=pat, pst=pst: e.tensor_add(
                                    pst[:, j * 512:(j + 1) * 512],
                                    pst[:, j * 512:(j + 1) * 512],
                                    self.masks[:, pat * 512:(pat + 1) * 512]),
                                    waits=[ev_mm, self.ev_const], inc=self.V)
                                ev_masks.append(ev_mk)
                        ev_mm_pair = self.tev()
                        width = 512 * len(pair)
                        wexp = [ev_mm_pair, self.ev_constv] + ev_masks
                        if at_free[slot] is not None:
                            wexp.append(at_free[slot])
                        ev_e = o("s", lambda e, at_sl=at_sl, pst=pst, width=width: e.activation(
                            at_sl[:, 0:width], pst[:, 0:width], AF.Exp,
                            bias=self.neg30[:], scale=0.125), waits=wexp, inc=self.A)
                        slot_free[slot] = ev_e
                        for j, (kt, pat) in enumerate(pair):
                            first = (pi == 0 and j == 0)
                            last = (pi * 2 + j == nent - 1)
                            wav = [ev_e]
                            if first and ev_prev_ctx is not None:
                                wav.append(ev_prev_ctx)
                            o("t", lambda e, kt=kt, j=j, at_sl=at_sl, vtile=vtile,
                                first=first, last=last: e.matmul(
                                self.pacc[0:65, 0:512],
                                vtile[:, kt * 65: kt * 65 + 65],
                                at_sl[:, j * 512:(j + 1) * 512],
                                start=first, stop=last),
                                waits=wav if j == 0 else [], inc=self.T)
                        at_free[slot] = self.tev()
                        ev_av_last = self.tev()
                    ev_sr = o("v", lambda e: e.tensor_copy(self.sumrow[:], self.pacc[64:65, 0:512]),
                              waits=[ev_av_last], inc=self.V)
                    ev_rc = o("v", lambda e: e.reciprocal(self.sumrec[:], self.sumrow[:]),
                              waits=[ev_sr], inc=self.V)
                    ev_bc = o("t", lambda e: e.matmul(
                        self.pacc[0:64, 512:1024], self.ones[:], self.sumrec[:],
                        start=True, stop=True), waits=[ev_rc, self.ev_constv], inc=self.T)
                    ev_bs = o("v", lambda e: e.tensor_copy(self.bcast[:], self.pacc[0:64, 512:1024]),
                              waits=[ev_bc], inc=self.V)
                    ev_cx = o("v", lambda e, g=g, hb=hb, qb=qb: e.tensor_mul(
                        self.ctx[hb:hb + 64, g * S + qb * 512: g * S + (qb + 1) * 512],
                        self.pacc[0:64, 0:512], self.bcast[:]),
                        waits=[ev_bs], inc=self.V)
                    ev_ctx_last = ev_cx
            ev_group_done = self.vev()
        self.ev_attn_done = ev_ctx_last
        if layer == 0 and "ctx0" in self.dbg_d:
            o("y", lambda e: e.dma_start(out=self.dbg_d["ctx0"][:], in_=self.ctx[:]),
              waits=[ev_ctx_last], inc=self.dY)

        ev_wo = o("g", lambda e, wo_sb=wo_sb, wo_d=wo_d: e.dma_start(
            out=wo_sb[:], in_=wo_d.rearrange("(cc p) o -> p cc o", p=128)),
            waits=[self.ev_attn_done], inc=self.aw)
        ctxv = self.ctx.rearrange("p (c n) -> p c n", c=4)
        ev_store_last = None
        ps_free = {0: None, 1: None}
        sl_free = {0: None, 1: None}
        for oh in range(2):
            for tt in range(NT_SEQ):
                sl = (tt * 2 + oh) % 2
                ps = self.pacc[:, sl * 512:(sl + 1) * 512]
                wmm = [ev_wo, self.ev_attn_done]
                if ps_free[sl] is not None:
                    wmm.append(ps_free[sl])
                for cc in range(4):
                    o("t", lambda e, cc=cc, tt=tt, oh=oh, ps=ps, ctxv=ctxv, wo_sb=wo_sb: e.matmul(
                        ps, ctxv[:, cc, tt * 128:(tt + 1) * 128],
                        wo_sb[:, cc * 1024 + oh * 512: cc * 1024 + (oh + 1) * 512],
                        start=(cc == 0), stop=(cc == 3)),
                        waits=wmm if cc == 0 else [], inc=self.T)
                ev_mm = self.tev()
                wv = [ev_mm]
                if sl_free[sl] is not None:
                    wv.append(sl_free[sl])
                ev_ev = o("v", lambda e, sl=sl, ps=ps, s_out=s_out: e.tensor_copy(
                    s_out[:, sl * 512:(sl + 1) * 512], ps), waits=wv, inc=self.V)
                ps_free[sl] = ev_ev
                ev_st = o("y", lambda e, tt=tt, oh=oh, sl=sl, cc_in=cc_in, s_out=s_out: e.dma_start(
                    out=cc_in[oh][tt * 128:(tt + 1) * 128, :],
                    in_=s_out[:, sl * 512:(sl + 1) * 512]), waits=[ev_ev], inc=self.dY)
                sl_free[sl] = ev_st
                ev_store_last = ev_st
            o("g", lambda e, oh=oh, cc_in=cc_in, cc_out=cc_out: e.collective_compute(
                "ReduceScatter", ALU.add, replica_groups=PAIR_GROUPS,
                ins=[cc_in[oh][:]], outs=[cc_out[oh][:]]),
                waits=[ev_store_last], inc=cchan, drain=False)
        if dbg_name in self.dbg_d:
            for oh in range(2):
                o("y", lambda e, oh=oh, cc_in=cc_in, dbg_name=dbg_name: e.dma_start(
                    out=self.dbg_d[dbg_name][:, oh * 512:(oh + 1) * 512].rearrange(
                        "a b -> a b") if False else
                    self.dbg_d[dbg_name].rearrange("a (h c) -> a h c", h=2)[:, oh, :],
                    in_=cc_in[oh][:]), waits=[ev_store_last], inc=self.dY)
        self.ev_cc = (cchan, cchan.n)

    def gating(self, layer, xn_hi, xn_lo, wg_d, ev_xnown):
        o = self.P.op
        ev_w = o("y", lambda e, wg_d=wg_d: e.dma_start(
            out=self.wg32[:], in_=wg_d.rearrange("(kk p) ee -> p kk ee", p=128)),
            waits=[ev_xnown], inc=self.dY)
        ev_hi = o("v", lambda e: e.tensor_copy(self.wg_hi[:], self.wg32[:]),
                  waits=[ev_w], inc=self.V)
        ev_lo = o("v", lambda e: e.scalar_tensor_tensor(
            self.wg_lo[:], self.wg_hi[:], -1.0, self.wg32[:], op0=ALU.mult, op1=ALU.add),
            waits=[ev_hi], inc=self.V)
        if layer == 0 and "wghl" in self.dbg_d:
            ev_lo = o("v", lambda e: e.tensor_copy(self.gdbg16[:, 0:64], self.wg_hi[:]),
                      waits=[ev_lo], inc=self.V)
            ev_lo = o("v", lambda e: e.tensor_copy(self.gdbg16[:, 64:128], self.wg_lo[:]),
                      waits=[ev_lo], inc=self.V)
        xnh = xn_hi.rearrange("p (c n) -> p c n", c=DCH)
        xnl = xn_lo.rearrange("p (c n) -> p c n", c=DCH)
        ev_cw = None
        ev_plog_free = None
        for tt in range(NT_OWN):
            wmm = [ev_lo, ev_xnown]
            if ev_plog_free is not None:
                wmm.append(ev_plog_free)
            nmm = 0
            for whi, xsrc in ((self.wg_hi, xnh), (self.wg_lo, xnh), (self.wg_hi, xnl)):
                for kk in range(DCH):
                    nmm += 1
                    o("t", lambda e, kk=kk, tt=tt, whi=whi, xsrc=xsrc, nmm=nmm: e.matmul(
                        self.pb0[:, 0:8], xsrc[:, kk, tt * 128:(tt + 1) * 128],
                        whi[:, kk * 8:(kk + 1) * 8],
                        start=(nmm == 1), stop=(nmm == 24)),
                        waits=wmm if nmm == 1 else [], inc=self.T)
            ev_mm = self.tev()
            L, Gt, st = self.logits, self.gtmp, self.st
            ev = o("v", lambda e, L=L: e.tensor_copy(L[:], self.pb0[:, 0:8]),
                   waits=[ev_mm], inc=self.V)
            ev_plog_free = ev
            if layer == 0 and "glog" in self.dbg_d:
                ev = o("v", lambda e, L=L, tt=tt: e.tensor_copy(
                    self.gdbg[:, tt * 8:(tt + 1) * 8], L[:]), waits=[ev], inc=self.V)
            ev_m1 = o("v", lambda e, L=L: e.reduce_max(st[:, 8:9], L[:], axis=AX.X, negate=True),
                      waits=[ev], inc=self.V)
            ev_exp = o("s", lambda e, L=L, Gt=Gt: e.activation(Gt[:], L[:], AF.Exp,
                                                               bias=st[:, 8:9]),
                       waits=[ev_m1], inc=self.A)
            if layer == 1:
                ev_z = o("v", lambda e, Gt=Gt: e.reduce_sum(st[:, 12:13], Gt[:], axis=AX.X),
                         waits=[ev_exp], inc=self.V)
            ev = o("v", lambda e, L=L: e.tensor_scalar(
                self.gsel[:], L[:], st[:, 8:9], 0.0, op0=ALU.add, op1=ALU.is_ge),
                waits=[ev_m1], inc=self.V)
            ev = o("v", lambda e, L=L: e.scalar_tensor_tensor(
                self.gmask[:], self.gsel[:], NEG_BIG, L[:], op0=ALU.mult, op1=ALU.add),
                waits=[ev], inc=self.V)
            ev = o("v", lambda e: e.reduce_max(st[:, 9:10], self.gmask[:], axis=AX.X, negate=True),
                   waits=[ev], inc=self.V)
            ev = o("v", lambda e, L=L: e.tensor_scalar(
                self.gsel[:], L[:], st[:, 9:10], 0.0, op0=ALU.add, op1=ALU.is_ge),
                waits=[ev], inc=self.V)
            ev_sel = o("v", lambda e, Gt=Gt: e.tensor_mul(Gt[:], Gt[:], self.gsel[:]),
                       waits=[ev, ev_exp], inc=self.V)
            ev_s2 = o("v", lambda e, Gt=Gt: e.reduce_sum(st[:, 10:11], Gt[:], axis=AX.X),
                      waits=[ev_sel], inc=self.V)
            if layer == 1:
                ev_s2 = o("v", lambda e: e.scalar_tensor_tensor(
                    st[:, 10:11], st[:, 12:13], 1e-9, st[:, 10:11], op0=ALU.mult, op1=ALU.add),
                    waits=[ev_s2, ev_z], inc=self.V)
            ev_rc = o("v", lambda e: e.reciprocal(st[:, 11:12], st[:, 10:11]),
                      waits=[ev_s2], inc=self.V)
            ev_cwt = o("v", lambda e, tt=tt, Gt=Gt: e.tensor_scalar(
                self.cw_all[:, tt * 8:(tt + 1) * 8], Gt[:], st[:, 11:12], None, op0=ALU.mult),
                waits=[ev_rc], inc=self.V)
            ev_tr = o("t", lambda e, tt=tt: e.transpose(
                self.pb1[0:8, 0:128], self.cw_all[:, tt * 8:(tt + 1) * 8],
                self.ident32[:]), waits=[ev_cwt, self.ev_const], inc=self.T)
            ev_cw = o("v", lambda e, tt=tt: e.tensor_copy(
                self.cw_fm[:, tt * 128:(tt + 1) * 128], self.pb1[0:8, 0:128]),
                waits=[ev_tr], inc=self.V)
        return ev_cw

    def moe_experts(self, layer, xn_hi, acc, acc_init_first, ev_cw, ev_xnown, init_guard,
                    init_guard_dma=None):
        self.moe_tb_done = [None] * 4
        o = self.P.op
        F = F0 if layer == 0 else F1
        nsg = F // SG
        w1d = self.l0_w1 if layer == 0 else self.l1_w1
        w3d = self.l0_w3 if layer == 0 else self.l1_w3
        w2d = self.l0_w2 if layer == 0 else self.l1_w2
        xnh = xn_hi.rearrange("p (c n) -> p c n", c=DCH)
        wchan = {0: self.wS0, 1: self.wS1}
        self._ig_dma = init_guard_dma
        ev_slot_use = {0: init_guard, 1: init_guard}
        ev_load = {}
        loads = [(e, sg) for e in range(NE) for sg in range(nsg)]

        def rec_load(idx):
            e, sg = loads[idx]
            slot = idx % 2
            w = [ev_slot_use[slot]] if ev_slot_use[slot] is not None else []
            if idx < 2 and self._ig_dma is not None:
                w.append(self._ig_dma)
            o("g", lambda en, e=e, sg=sg, slot=slot, w1d=w1d: en.dma_start(
                out=self.w1_sb[slot][:],
                in_=w1d.rearrange("E (kk p) (sgn c) -> E p kk sgn c", p=128, c=SG)[e][:, :, sg]),
                waits=w, inc=wchan[slot])
            o("g", lambda en, e=e, sg=sg, slot=slot, w3d=w3d: en.dma_start(
                out=self.w3_sb[slot][:],
                in_=w3d.rearrange("E (kk p) (sgn c) -> E p kk sgn c", p=128, c=SG)[e][:, :, sg]),
                inc=wchan[slot])
            o("g", lambda en, e=e, sg=sg, slot=slot, w2d=w2d: en.dma_start(
                out=self.w2_sb[slot][:],
                in_=w2d.rearrange("E (sgn hh p) o -> E p sgn hh o", p=128, hh=4)[e][:, sg]),
                inc=wchan[slot])
            ev_load[idx] = (wchan[slot], wchan[slot].n)

        rec_load(0)
        if len(loads) > 1:
            rec_load(1)

        ev_cwb = None
        ev_cwb_war = None
        ev_ab_free = {0: None, 1: None}
        h_free = None
        pacc_free = {0: None, 1: None, 2: None}
        ev_acc_tail = None
        for idx, (e, sg) in enumerate(loads):
            slot = idx % 2
            for tb in range(4):
                wcb = [ev_cw, self.ev_constv]
                if ev_cwb_war is not None:
                    wcb.append(ev_cwb_war)
                ev_mm = o("t", lambda en, e=e, tb=tb: en.matmul(
                    self.pb1[:, 0:512], self.onehot[:, e * 128:(e + 1) * 128],
                    self.cw_fm[:, tb * 512:(tb + 1) * 512], start=True, stop=True),
                    waits=wcb, inc=self.T)
                ev_cwb = o("v", lambda en: en.tensor_copy(self.cwb[:], self.pb1[:, 0:512]),
                           waits=[ev_mm], inc=self.V)
                for hh in range(4):
                    abs_ = self.ab_a[hh % 2]
                    bbs = self.ab_b[hh % 2]
                    pband = self.pb0 if hh % 2 == 0 else self.pb1
                    wmm = [ev_load[idx], ev_xnown, ev_cw]
                    if hh % 2 == 1:
                        wmm.append(ev_cwb)
                    if ev_ab_free[hh % 2] is not None:
                        wmm.append(ev_ab_free[hh % 2])
                    for kk in range(DCH):
                        o("t", lambda en, kk=kk, hh=hh, tb=tb, slot=slot, xnh=xnh,
                            pband=pband: en.matmul(
                            pband[:, 0:512],
                            self.w1_sb[slot][:, kk * SG + hh * 128: kk * SG + (hh + 1) * 128],
                            xnh[:, kk, tb * 512:(tb + 1) * 512],
                            start=(kk == 0), stop=(kk == DCH - 1)),
                            waits=wmm if kk == 0 else [], inc=self.T)
                    for kk in range(DCH):
                        o("t", lambda en, kk=kk, hh=hh, tb=tb, slot=slot, xnh=xnh,
                            pband=pband: en.matmul(
                            pband[:, 512:1024],
                            self.w3_sb[slot][:, kk * SG + hh * 128: kk * SG + (hh + 1) * 128],
                            xnh[:, kk, tb * 512:(tb + 1) * 512],
                            start=(kk == 0), stop=(kk == DCH - 1)), inc=self.T)
                    ev_mm = self.tev()
                    ev_sil = o("s", lambda en, abs_=abs_, pband=pband: en.activation(
                        abs_[:], pband[:, 0:512], AF.Silu), waits=[ev_mm], inc=self.A)
                    ev_bs = o("v", lambda en, bbs=bbs, pband=pband: en.tensor_mul(
                        bbs[:], pband[:, 512:1024], self.cwb[:]),
                        waits=[ev_mm, ev_cwb], inc=self.V)
                    wh = [ev_sil, ev_bs]
                    if h_free is not None and hh < 2:
                        wh.append(h_free)
                    ev_h = o("v", lambda en, hh=hh, abs_=abs_, bbs=bbs: en.tensor_mul(
                        self.h_sb[:, hh * 512:(hh + 1) * 512], abs_[:], bbs[:]),
                        waits=wh, inc=self.V)
                    ev_ab_free[hh % 2] = ev_h
                    ev_cwb_war = ev_h
                ev_h_all = ev_h
                if idx == 0 and tb == 0 and "h00" in self.dbg_d and layer == 0:
                    o("y", lambda en: en.dma_start(
                        out=self.dbg_d["h00"][:], in_=self.h_sb[:]),
                        waits=[ev_h_all], inc=self.dY)
                    o("y", lambda en: en.dma_start(
                        out=self.dbg_d["cwb0"][:, 0:512], in_=self.cwb[:]),
                        waits=[ev_cwb], inc=self.dY)
                for ttl in range(4):
                    for oh in range(2):
                        psl = (ttl * 2 + oh) % 3
                        ps = (self.pacc[:, 0:512], self.pacc[:, 512:1024],
                              self.paux7[:])[psl]
                        wmm = [ev_h_all, ev_load[idx]]
                        if pacc_free[psl] is not None:
                            wmm.append(pacc_free[psl])
                        for hh in range(4):
                            o("t", lambda en, hh=hh, ttl=ttl, oh=oh, slot=slot, ps=ps: en.matmul(
                                ps, self.h_sb[:, hh * 512 + ttl * 128: hh * 512 + (ttl + 1) * 128],
                                self.w2_sb[slot][:, hh * 1024 + oh * 512: hh * 1024 + (oh + 1) * 512],
                                start=(hh == 0), stop=(hh == 3)),
                                waits=wmm if hh == 0 else [], inc=self.T)
                        ev_mm = self.tev()
                        col = (tb * 4 + ttl) * 1024 + oh * 512
                        if acc_init_first and idx == 0:
                            ev_acc = o("v", lambda en, col=col, ps=ps, acc=acc: en.tensor_copy(
                                acc[:, col:col + 512], ps), waits=[ev_mm], inc=self.V)
                        else:
                            ev_acc = o("v", lambda en, col=col, ps=ps, acc=acc: en.tensor_add(
                                acc[:, col:col + 512], acc[:, col:col + 512], ps),
                                waits=[ev_mm], inc=self.V)
                        pacc_free[psl] = ev_acc
                        ev_acc_tail = ev_acc
                if idx == len(loads) - 1:
                    self.moe_tb_done[tb] = ev_acc_tail
                h_free = self.tev()
            ev_slot_use[slot] = self.tev()
            if idx + 2 < len(loads):
                rec_load(idx + 2)
        return ev_acc_tail

    def combine(self, x_src, s_src, ev_s_ready, g_src_tile_fn, ev_g_ready, dst, dchan,
                guard=None):
        o = self.P.op
        ev_last = None
        ev_xs_free = None
        for tt in range(NT_OWN):
            w = [guard] if guard is not None else []
            if ev_xs_free is not None:
                w.append(ev_xs_free)
            ev_x = o("s", lambda e, tt=tt, x_src=x_src: e.dma_start(
                out=self.xs[:], in_=x_src[tt * 128:(tt + 1) * 128, :]), waits=w, inc=self.dS)
            ev_s = None
            for oh in range(2):
                ev_s = o("y", lambda e, tt=tt, oh=oh, s_src=s_src: e.dma_start(
                    out=self.comb_s[:, oh * 512:(oh + 1) * 512],
                    in_=s_src[oh][tt * 128:(tt + 1) * 128, :]),
                    waits=[ev_s_ready] + w, inc=self.dY)
            ev_g = g_src_tile_fn(tt, w)
            ev_t = o("v", lambda e: e.tensor_mul(self.comb_t[:], self.comb_g[:], self.comb_s[:]),
                     waits=[ev_g, ev_s], inc=self.V)
            ev_o = o("v", lambda e: e.tensor_add(self.xs[:], self.xs[:], self.comb_t[:]),
                     waits=[ev_t, ev_x], inc=self.V)
            eng = "s" if dchan is self.dOUT else "y"
            ev_st = o(eng, lambda e, tt=tt, dst=dst: e.dma_start(
                out=dst[tt * 128:(tt + 1) * 128, :], in_=self.xs[:]), waits=[ev_o], inc=dchan)
            ev_xs_free = ev_st
            self.p5_tile_evs = getattr(self, "p5_tile_evs", [])
            if tt == 0:
                self.p5_tile_evs = []
            self.p5_tile_evs.append(ev_st)
            ev_last = ev_st
        return ev_last

    def finish(self):
        o = self.P.op
        waits = [(c, c.n) for c in (self.dS, self.dY, self.aw, self.wS0, self.wS1,
                                    self.dOUT, self.CC0, self.CC1, self.CC2)
                 if c.n > 0]
        o("y", lambda e: e.nop(), waits=waits)
        self.P.emit(self.nc)
        return self.nc

    def build(self):
        o = self.P.op
        self.load_constants()

        self.ln_transpose_full(self.x_full, NT_SEQ, self.xn_att, False, None,
                               rms=False, eps=self.eps5)
        ev_xn_att = self.vev()
        if "xn0" in self.dbg_d:
            o("y", lambda e: e.dma_start(out=self.dbg_d["xn0"][:], in_=self.xn_att[:]),
              waits=[ev_xn_att], inc=self.dY)
        if self.upto == "p1":
            return self.finish()

        self.attention_v2(0, ev_xn_att, ev_w_guard=self.ev_const,
                          ev_xn_tiles=list(self.ln_tile_evs))
        ev_cc0 = self.ev_cc
        if self.upto == "attn0":
            return self.finish()

        self.ln_transpose_full(self.x_own, NT_OWN, self.xn_own, True, self.xn_lo,
                               rms=False, eps=self.eps5,
                               guard=[self.ev_attn_done])
        ev_xnown = self.vev()
        if "xnown" in self.dbg_d:
            o("y", lambda e: e.dma_start(out=self.dbg_d["xnown"][:], in_=self.xn_own[:]),
              waits=[ev_xnown], inc=self.dY)
        if "xnlo" in self.dbg_d:
            o("y", lambda e: e.dma_start(out=self.dbg_d["xnlo"][:], in_=self.xn_lo[:]),
              waits=[ev_xnown], inc=self.dY)
        ev_cw = self.gating(0, self.xn_own, self.xn_lo, self.l0_wg, ev_xnown)
        if "glog" in self.dbg_d:
            o("y", lambda e: e.dma_start(out=self.dbg_d["glog"][:], in_=self.gdbg[:]),
              waits=[ev_cw], inc=self.dY)
        if "wghl" in self.dbg_d:
            o("y", lambda e: e.dma_start(out=self.dbg_d["wghl"][:], in_=self.gdbg16[:]),
              waits=[ev_cw], inc=self.dY)
        if "cwall" in self.dbg_d:
            o("y", lambda e: e.dma_start(out=self.dbg_d["cwall"][:], in_=self.cw_all[:]),
              waits=[ev_cw], inc=self.dY)
        if "cwfm" in self.dbg_d:
            o("y", lambda e: e.dma_start(out=self.dbg_d["cwfm"][:], in_=self.cw_fm[:]),
              waits=[ev_cw], inc=self.dY)
        ev_moe = self.moe_experts(0, self.xn_own, self.acc, True, ev_cw, ev_xnown,
                                  init_guard=self.tev(),
                                  init_guard_dma=(self.dY, self.dY.n))
        if "acc0" in self.dbg_d:
            o("y", lambda e: e.dma_start(out=self.dbg_d["acc0"][:], in_=self.acc[:]),
              waits=[ev_moe], inc=self.dY)
        if self.upto == "moe0":
            return self.finish()

        def g_tile_l0(tt, w):
            return o("s", lambda e, tt=tt: e.activation(
                self.comb_g[:], self.acc[:, tt * 1024:(tt + 1) * 1024], AF.Silu),
                waits=[ev_moe] + w, inc=self.A)
        ev_x1 = self.combine(self.x_own, self.cc_s0_out, ev_cc0, g_tile_l0, ev_moe,
                             self.cc_x1_in, self.dY, guard=ev_moe)
        if "x1" in self.dbg_d:
            ev_x1 = o("y", lambda e: e.dma_start(
                out=self.dbg_d["x1"][:], in_=self.cc_x1_in[:]), waits=[ev_x1], inc=self.dY)
        o("g", lambda e: e.collective_compute(
            "AllGather", ALU.bypass, replica_groups=PAIR_GROUPS,
            ins=[self.cc_x1_in[:]], outs=[self.cc_x1_out[:]]), waits=[ev_x1], inc=self.CC1,
          drain=False)
        ev_cc1 = (self.CC1, self.CC1.n)
        if self.upto == "x1":
            return self.finish()

        ev_z = self.ln_full_f32(self.cc_x1_in, self.z, guard=[ev_x1],
                                per_tile_guards=self.p5_tile_evs)
        z_tile_evs = list(self.ln_z_tile_evs)
        for loop in range(2):
            ev_ev = None
            ev_ev2 = None
            ev_xs_free = None
            for tt in range(NT_OWN):
                w = [z_tile_evs[tt]]
                if ev_xs_free is not None:
                    w.append(ev_xs_free)
                ev_x = o("v", lambda e, tt=tt: e.tensor_copy(
                    self.xs[:], self.z[:, tt * 1024:(tt + 1) * 1024]), waits=w, inc=self.V)
                evs = self.ln_tile(ev_x, self.eps6, True, rms=True)
                ev_ev = self.transpose_tile(evs["hi"], self.xntm, self.zn_hi, tt * 128, 0, ev_ev)
                ev_ev2 = self.transpose_tile(evs["lo"], self.xlotm, self.xn_lo, tt * 128, 0, ev_ev)
                ev_ev = ev_ev2
                ev_xs_free = (self.V, max(ev_ev2[1], evs["xs_free"][1]))
            ev_xnown = self.vev()
            ev_cw = self.gating(1, self.zn_hi, self.xn_lo, self.l1_rt, ev_xnown)
            ev_z = self.moe_experts(1, self.zn_hi, self.z, False, ev_cw, ev_xnown,
                                    init_guard=self.tev())
            z_tile_evs = [self.moe_tb_done[tt // 4] for tt in range(NT_OWN)]
        if "z" in self.dbg_d:
            o("y", lambda e: e.dma_start(out=self.dbg_d["z"][:], in_=self.z[:]),
              waits=[ev_z], inc=self.dY)

        ev_g1 = None
        for tt in range(NT_OWN):
            ev_g1 = o("s", lambda e, tt=tt: e.activation(
                self.g1[:, tt * 1024:(tt + 1) * 1024],
                self.z[:, tt * 1024:(tt + 1) * 1024], AF.Silu), waits=[ev_z], inc=self.A)
        if "g1" in self.dbg_d:
            o("y", lambda e: e.dma_start(out=self.dbg_d["g1"][:], in_=self.g1[:]),
              waits=[ev_g1], inc=self.dY)
        if self.upto == "ffn1":
            return self.finish()

        np1 = self.mask1_pat.shape[0]
        ev_mask1 = o("y", lambda e, np1=np1: e.dma_start(
            out=self.masks[:, 0:np1 * 512], in_=self.m1_d.rearrange("n p q -> p n q")),
            waits=[self.ev_attn_done, (self.T, self.T.n), (self.V, self.V.n)], inc=self.dY)
        self.ev_const = (self.dY, self.dY.n)
        self.ln_transpose_full(self.cc_x1_out, NT_SEQ, self.xn_att, False, None,
                               rms=False, eps=self.eps5, guard=[ev_g1, ev_cc1])
        ev_xn1 = self.vev()
        if "xn1" in self.dbg_d:
            o("y", lambda e: e.dma_start(out=self.dbg_d["xn1"][:], in_=self.xn_att[:]),
              waits=[ev_xn1], inc=self.dY)
        self.attention_v2(1, ev_xn1, ev_w_guard=self.tev(),
                          ev_xn_tiles=list(self.ln_tile_evs))
        ev_cc2 = self.ev_cc

        def g_tile_l1(tt, w):
            return o("v", lambda e, tt=tt: e.tensor_copy(
                self.comb_g[:], self.g1[:, tt * 1024:(tt + 1) * 1024]),
                waits=[ev_g1] + w, inc=self.V)
        self.combine(self.cc_x1_in, self.cc_s1_out, ev_cc2, g_tile_l1, ev_g1,
                     self.out_own, self.dOUT, guard=ev_cc2)
        o("y", lambda e: e.nop(), waits=[(self.dOUT, self.dOUT.n)])

        self.P.emit(self.nc)
        return self.nc


_BUILD_CACHE = {}


def get_program(dbg=(), upto="all"):
    key = (tuple(sorted(dbg)), upto)
    if key not in _BUILD_CACHE:
        b = Builder(dbg=dbg, upto=upto)
        nc = b.build()
        _BUILD_CACHE[key] = (nc, b)
    return _BUILD_CACHE[key]


def prep_inputs(inp):
    f = lambda a: np.ascontiguousarray(np.asarray(a), dtype=np.float32)
    x = f(inp["L0_attn_x"])
    Wq, Wk, Wv, Wo = (f(inp["L0_attn_Wq"]), f(inp["L0_attn_Wk"]),
                      f(inp["L0_attn_Wv"]), f(inp["L0_attn_Wo"]))
    wg = f(inp["L0_ffn_wg"])
    w1, w2, w3 = f(inp["L0_ffn_w1"]), f(inp["L0_ffn_w2"]), f(inp["L0_ffn_w3"])
    w_attn, w_proj = f(inp["L1_attn_w_attn"]), f(inp["L1_attn_w_proj"])
    norm_w = f(inp["L1_ffn_norm_w"])
    router = f(inp["L1_ffn_router_w"])
    v1, v3, v2 = f(inp["L1_ffn_w1"]), f(inp["L1_ffn_w3"]), f(inp["L1_ffn_w2"])

    ct, st = gen_rope_tables()
    m0, _ = gen_schedule(l0_mask_bool())
    m1, _ = gen_schedule(l1_mask_bool())
    ident16 = np.eye(128, dtype=BF)
    ident32 = np.eye(128, dtype=np.float32)
    onehot = np.zeros((8, 8 * 128), dtype=np.float32)
    for e in range(8):
        onehot[e, e * 128:(e + 1) * 128] = 1.0
    normw_b = np.ascontiguousarray(np.broadcast_to(norm_w[None, :], (128, D)))

    swap_idx = np.concatenate([np.arange(64).reshape(2, 32)[::-1].ravel() + j * 64
                               for j in range(NH)])
    WqT, WkT, WvT, WoT = Wq.T, Wk.T, Wv.T, Wo.T
    WqTs = WqT[:, swap_idx]
    WkTs = WkT[:, swap_idx]
    w_attn_T = w_attn.T
    w_projT = w_proj.T

    shared = {
        "ident16": ident16, "ident32": ident32,
        "rope_cos": ct, "rope_sin": st,
        "l0_masks": m0, "l1_masks": m1,
        "onehot": onehot, "l1_normw_bcast": normw_b,
        "l0_wg_t": np.ascontiguousarray(wg.T),
        "l0_w1_t": np.ascontiguousarray(w1.transpose(0, 2, 1)),
        "l0_w3_t": np.ascontiguousarray(w3.transpose(0, 2, 1)),
        "l0_w2_t": np.ascontiguousarray(w2.transpose(0, 2, 1)),
        "l1_router_t": np.ascontiguousarray(router),
        "l1_w1": v1, "l1_w3": v3, "l1_w2": v2,
    }
    maps = []
    for core in range(8):
        b_, r = core // 2, core % 2
        cols = slice(r * 512, (r + 1) * 512)
        m = dict(shared)
        m["x_full"] = np.ascontiguousarray(x[b_])
        m["x_own"] = np.ascontiguousarray(x[b_, r * TOK:(r + 1) * TOK])
        m["l0_wq_t"] = np.ascontiguousarray(WqT[:, cols])
        m["l0_wq_swap_t"] = np.ascontiguousarray(WqTs[:, cols])
        m["l0_wk_t"] = np.ascontiguousarray(WkT[:, cols])
        m["l0_wk_swap_t"] = np.ascontiguousarray(WkTs[:, cols])
        m["l0_wv_t"] = np.ascontiguousarray(WvT[:, cols])
        m["l0_wo_my"] = np.ascontiguousarray(WoT[cols, :])
        m["l1_wq_t"] = np.ascontiguousarray(w_attn_T[:, 0 + r * 512: 0 + (r + 1) * 512])
        m["l1_wk_t"] = np.ascontiguousarray(w_attn_T[:, 1024 + r * 512: 1024 + (r + 1) * 512])
        m["l1_wv_t"] = np.ascontiguousarray(w_attn_T[:, 2048 + r * 512: 2048 + (r + 1) * 512])
        m["l1_wo_my"] = np.ascontiguousarray(w_projT[cols, :])
        maps.append(m)
    return maps


def kernel(**inputs):
    nc, _b = get_program()
    maps = prep_inputs(inputs)
    res = run_bass_kernel_spmd(nc, maps, core_ids=list(range(8)))
    out = np.empty((B, S, D), dtype=np.float32)
    for core in range(8):
        b_, r = core // 2, core % 2
        out[b_, r * TOK:(r + 1) * TOK] = np.asarray(res.results[core]["out_own"])
    return out


# version 12
# speedup vs baseline: 1.0153x; 1.0153x over previous
import numpy as np
import ml_dtypes
import concourse.bass as bass
import concourse.mybir as mybir
from concourse.bass_utils import run_bass_kernel_spmd

BF = ml_dtypes.bfloat16
F32 = mybir.dt.float32
BF16 = mybir.dt.bfloat16
AF = mybir.ActivationFunctionType
ALU = mybir.AluOpType
AX = mybir.AxisListType

B, S, D, NH, HD = 4, 4096, 1024, 16, 64
TOK = 2048
NT_OWN = TOK // 128
NT_SEQ = S // 128
DCH = D // 128
NQB = S // 512
NKT = S // 128
F0, F1 = 4096, 2048
NE = 8
SG = 512
NEG_BIG = -1.0e30
EXP_SHIFT = -30.0
PAIR_GROUPS = [[0, 1], [2, 3], [4, 5], [6, 7]]



def gen_rope_tables():
    half = HD // 2
    inv_freq = 1.0 / (10000.0 ** (np.arange(half, dtype=np.float64) / half))
    ang = np.arange(S, dtype=np.float64)[:, None] * inv_freq[None, :]
    cos = np.cos(ang).astype(np.float32)
    sin = np.sin(ang).astype(np.float32)
    ct = np.empty((128, S), dtype=np.float32)
    st = np.empty((128, S), dtype=np.float32)
    for rr in range(128):
        j = rr % half
        ct[rr] = cos[:, j]
        st[rr] = sin[:, j] if (rr % HD) >= half else -sin[:, j]
    return ct, st


def l0_mask_bool():
    i = np.arange(S)[:, None]
    j = np.arange(S)[None, :]
    return (j <= i)


def l1_mask_bool():
    LOCAL = 256
    GIDX = np.array([0, 1, 2, 3])
    i = np.arange(S)[:, None]
    j = np.arange(S)[None, :]
    causal = j <= i
    local = causal & (j >= i - LOCAL + 1)
    g_row = np.isin(np.arange(S), GIDX)[:, None]
    g_col = np.isin(np.arange(S), GIDX)[None, :]
    return local | (g_row & causal) | (g_col & causal)


def gen_schedule(mask):
    patterns = []
    pat_key = {}
    sched = []
    for qb in range(NQB):
        row = []
        for kt in range(NKT):
            tile = mask[qb * 512:(qb + 1) * 512, kt * 128:(kt + 1) * 128].T
            if not tile.any():
                continue
            if tile.all():
                row.append((kt, None))
                continue
            key = tile.tobytes()
            if key not in pat_key:
                pat_key[key] = len(patterns)
                patterns.append(np.where(tile, 0.0, NEG_BIG).astype(np.float32))
            row.append((kt, pat_key[key]))
        sched.append(row)
    return np.stack(patterns), sched



class Chan:
    __slots__ = ("name", "step", "n", "sem", "producer")

    def __init__(self, name, step):
        self.name, self.step, self.n, self.sem = name, step, 0, None
        self.producer = None


class Prog:
    ENGINES = ("y", "g", "t", "v", "s")
    SECTION = {"y": "sync", "g": "gpsimd", "t": "tensor", "v": "vector", "s": "scalar"}

    def __init__(self):
        self.streams = {e: [] for e in self.ENGINES}
        self.chans = {}

    def chan(self, name, step=1):
        c = Chan(name, step)
        self.chans[name] = c
        return c

    def op(self, eng, fn, waits=(), inc=None, drain=None):
        ws = []
        for w in waits:
            if w is None:
                continue
            c, v = w
            assert v <= c.n, f"wait on future event {c.name}: {v} > {c.n}"
            if v > 0:
                ws.append((c, v))
        if inc is not None:
            if inc.producer is None:
                inc.producer = eng
            assert inc.producer == eng, f"chan {inc.name} inc from {eng} and {inc.producer}"
            inc.n += inc.step
        self.streams[eng].append((fn, tuple(ws), inc, drain, inc.n if inc else 0))
        return (inc, inc.n) if inc is not None else None

    def emit(self, nc):
        cross = {}
        for eng in self.ENGINES:
            for fn, ws, inc, drain, incv in self.streams[eng]:
                for c, v in ws:
                    if c.step == 1 and c.producer is not None and c.producer != eng:
                        cross.setdefault(c.name, set()).add(v)
        for c in self.chans.values():
            c.sem = nc.alloc_semaphore(c.name)
        with nc.Block() as block:
            for eng in self.ENGINES:
                stream = self.streams[eng]
                if not stream:
                    continue

                def make_body(stream=stream, eng=eng):
                    def body(engine):
                        waited = {}
                        for fn, ws, inc, drain, incv in stream:
                            for c, v in ws:
                                if waited.get(c.name, 0) < v:
                                    engine.wait_ge(c.sem, v)
                                    waited[c.name] = v
                            instr = fn(engine)
                            if inc is not None:
                                do_drain = drain
                                if do_drain is None:
                                    do_drain = (inc.step == 1 and eng in ("v", "s", "g")
                                                and incv in cross.get(inc.name, ()))
                                if do_drain:
                                    engine.drain().then_inc(inc.sem, inc.step)
                                else:
                                    instr.then_inc(inc.sem, inc.step)
                    return body

                getattr(block, self.SECTION[eng])(make_body())

    def counts(self):
        return {e: len(s) for e, s in self.streams.items()}



class Builder:
    def __init__(self, dbg=(), upto="all"):
        self.upto = upto
        self.dbg = set(dbg)
        self.nc = bass.Bass("TRN2", target_bir_lowering=False)
        self.P = Prog()
        ch = self.P.chan
        self.dS = ch("dS", 16)
        self.dY = ch("dY", 16)
        self.wS0 = ch("wS0", 16)
        self.wS1 = ch("wS1", 16)
        self.aw = ch("aw", 16)
        self.T = ch("T", 1)
        self.V = ch("V", 1)
        self.A = ch("A", 1)
        self.CC0 = ch("CC0", 1)
        self.CC1 = ch("CC1", 1)
        self.CC2 = ch("CC2", 1)
        self.dOUT = ch("dOUT", 16)

        self.mask0_pat, self.sched0 = gen_schedule(l0_mask_bool())
        self.mask1_pat, self.sched1 = gen_schedule(l1_mask_bool())
        assert self.mask0_pat.shape[0] <= 8, self.mask0_pat.shape
        assert self.mask1_pat.shape[0] <= 8, self.mask1_pat.shape

        self.declare_dram()
        self.alloc_sbuf()
        self.alloc_psum()

    def tev(self):
        return (self.T, self.T.n)

    def vev(self):
        return (self.V, self.V.n)

    def declare_dram(self):
        nc = self.nc
        DI = lambda n, s, d=F32: nc.dram_tensor(n, s, d, kind="ExternalInput")
        self.x_full = DI("x_full", [S, D])
        self.x_own = DI("x_own", [TOK, D])
        self.ident16_d = DI("ident16", [128, 128], BF16)
        self.ident32_d = DI("ident32", [128, 128], F32)
        self.ropec_d = DI("rope_cos", [128, S])
        self.ropes_d = DI("rope_sin", [128, S])
        self.m0_d = DI("l0_masks", list(self.mask0_pat.shape))
        self.m1_d = DI("l1_masks", list(self.mask1_pat.shape))
        self.onehot_d = DI("onehot", [8, 8 * 128])
        self.normw_d = DI("l1_normw_bcast", [128, D])

        self.l0_wq = DI("l0_wq_t", [D, 512])
        self.l0_wqs = DI("l0_wq_swap_t", [D, 512])
        self.l0_wk = DI("l0_wk_t", [D, 512])
        self.l0_wks = DI("l0_wk_swap_t", [D, 512])
        self.l0_wv = DI("l0_wv_t", [D, 512])
        self.l0_wo = DI("l0_wo_my", [512, D])
        self.l0_wg = DI("l0_wg_t", [D, 8])
        self.l0_w1 = DI("l0_w1_t", [NE, D, F0])
        self.l0_w3 = DI("l0_w3_t", [NE, D, F0])
        self.l0_w2 = DI("l0_w2_t", [NE, F0, D])

        self.l1_wq = DI("l1_wq_t", [D, 512])
        self.l1_wk = DI("l1_wk_t", [D, 512])
        self.l1_wv = DI("l1_wv_t", [D, 512])
        self.l1_wo = DI("l1_wo_my", [512, D])
        self.l1_rt = DI("l1_router_t", [D, 8])
        self.l1_w1 = DI("l1_w1", [NE, D, F1])
        self.l1_w3 = DI("l1_w3", [NE, D, F1])
        self.l1_w2 = DI("l1_w2", [NE, F1, D])

        DR = lambda n, s, d=F32: nc.dram_tensor(n, s, d, kind="Internal")
        self.cc_s0_in = [DR("cc_s0_inA", [S, 512], BF16), DR("cc_s0_inB", [S, 512], BF16)]
        self.cc_s0_out = [DR("cc_s0_outA", [TOK, 512], BF16), DR("cc_s0_outB", [TOK, 512], BF16)]
        self.cc_x1_in = DR("cc_x1_in", [TOK, D])
        self.cc_x1_out = DR("cc_x1_out", [S, D])
        self.cc_s1_in = [DR("cc_s1_inA", [S, 512], BF16), DR("cc_s1_inB", [S, 512], BF16)]
        self.cc_s1_out = [DR("cc_s1_outA", [TOK, 512], BF16), DR("cc_s1_outB", [TOK, 512], BF16)]

        self.out_own = nc.dram_tensor("out_own", [TOK, D], F32, kind="ExternalOutput")
        self.dbg_d = {}
        for name, shape, dt in (
            ("xn0", [128, DCH * S], BF16), ("acc0", [128, NT_OWN * D], F32),
            ("z", [128, NT_OWN * D], F32), ("g1", [128, NT_OWN * D], BF16),
            ("s0", [S, D], BF16), ("x1", [TOK, D], F32), ("s1", [S, D], BF16),
            ("xn1", [128, DCH * S], BF16), ("ctx0", [128, 4 * S], BF16),
            ("xnown", [128, DCH * TOK], BF16), ("xnlo", [128, DCH * TOK], BF16),
            ("cwall", [128, 128], F32), ("cwfm", [8, TOK], F32),
            ("cwb0", [128, TOK], BF16), ("h00", [128, 4 * 512], BF16),
            ("glog", [128, 128], F32), ("wghl", [128, 128], BF16),
        ):
            if name in self.dbg:
                self.dbg_d[name] = nc.dram_tensor("dbg_" + name, shape, dt, kind="ExternalOutput")

    def alloc_sbuf(self):
        nc = self.nc
        off = {}
        cur = 0

        def place(name, nbytes, align=32):
            nonlocal cur
            cur = (cur + align - 1) // align * align
            off[name] = cur
            cur += nbytes

        place("ident16", 256)
        place("ident32", 512)
        place("neg30", 32)
        place("eps5", 32)
        place("eps6", 32)
        place("st", 64)
        place("ones", 256)
        place("onehot", 4096)
        place("normw", 4096)
        place("gdbg", 512)
        place("gdbg16", 256)
        place("A", 65536)
        place("B", 32768)
        place("C", 32768)
        place("W", 57344)
        place("X", 12288)
        total = (cur + 31) // 32 * 32
        r = nc.bump_sbuf(total)
        assert r is not None, f"sbuf alloc failed: need {total}"
        base = r[0]

        def at(name, shape, dt, o):
            return nc.alloc_sbuf_tensor_at(name, shape, dt, offset=base + o)

        self.ident16 = at("s_ident16", [128, 128], BF16, off["ident16"])
        self.ident32 = at("s_ident32", [128, 128], F32, off["ident32"])
        self.neg30 = at("s_neg30", [128, 1], F32, off["neg30"])
        self.eps5 = at("s_eps5", [128, 1], F32, off["eps5"])
        self.eps6 = at("s_eps6", [128, 1], F32, off["eps6"])
        self.st = at("s_st", [128, 16], F32, off["st"])
        self.ones = at("s_ones", [1, 64], F32, off["ones"])
        self.onehot = at("s_onehot", [8, 8 * 128], F32, off["onehot"])
        self.normw = at("s_normw", [128, D], F32, off["normw"])

        A, Bo, C, W, X = off["A"], off["B"], off["C"], off["W"], off["X"]
        self.xn_att = at("s_xn_att", [128, DCH * S], BF16, A)
        self.acc = at("s_acc", [128, NT_OWN * D], F32, A)
        self.z = at("s_z", [128, NT_OWN * D], F32, A)
        self.qkv0 = dict(
            Q=at("s_q0Q", [128, S], BF16, Bo),
            K=at("s_q0K", [128, S], BF16, Bo + 8192),
            VA=at("s_q0VA", [128, NKT * 65], BF16, Bo + 16384),
            VB=at("s_q0VB", [128, NKT * 65], BF16, Bo + 20608),
            AT=at("s_q0AT", [128, 2048], BF16, Bo + 24832),
        )
        self.xn_own = at("s_xn_own", [128, DCH * TOK], BF16, Bo)
        self.zn_hi = at("s_zn_hi", [128, DCH * TOK], BF16, Bo)
        self.g1 = at("s_g1", [128, NT_OWN * D], BF16, Bo)
        self.ctx = at("s_ctx", [128, 4 * S], BF16, C)
        self.xn_lo = at("s_xn_lo", [128, DCH * TOK], BF16, C)
        self.cw_fm = at("s_cw_fm", [8, TOK], F32, W + 49152)
        self.cwb = at("s_cwb", [128, 512], BF16, X + 6144)
        self.ropeC = at("s_ropeC", [128, S], F32, W)
        self.ropeS = at("s_ropeS", [128, S], F32, W + 16384)
        self.masks = at("s_masks", [128, 8 * 512], F32, W + 32768)
        self.awq = at("s_awq", [128, DCH * 128], BF16, W + 49152)
        self.awqs = at("s_awqs", [128, DCH * 128], BF16, W + 51200)
        self.awk = at("s_awk", [128, DCH * 128], BF16, W + 53248)
        self.awks = at("s_awks", [128, DCH * 128], BF16, W + 55296)
        self.wo_sb = at("s_wo", [128, 4 * 1024], BF16, W)
        self.w1_sb = [at(f"s_w1_{i}", [128, DCH * SG], BF16, W + i * 24576) for i in range(2)]
        self.w3_sb = [at(f"s_w3_{i}", [128, DCH * SG], BF16, W + i * 24576 + 8192) for i in range(2)]
        self.w2_sb = [at(f"s_w2_{i}", [128, 4 * 1024], BF16, W + i * 24576 + 16384) for i in range(2)]
        self.qkv1 = dict(
            Q=at("s_q1Q", [128, S], BF16, W),
            K=at("s_q1K", [128, S], BF16, W + 8192),
            VA=at("s_q1VA", [128, NKT * 65], BF16, W + 16384),
            VB=at("s_q1VB", [128, NKT * 65], BF16, W + 20736),
            AT=at("s_q1AT", [128, 2048], BF16, W + 25024),
        )
        self.xs = at("s_xs", [128, D], F32, X)
        self.xq = at("s_xq", [128, D], F32, X + 4096)
        self.xntm = at("s_xntm", [128, D], BF16, X + 8192)
        self.xlotm = at("s_xlotm", [128, D], BF16, X + 10240)
        self.rope_t1 = at("s_ropet1", [128, 512], F32, X)
        self.rope_t2 = at("s_ropet2", [128, 512], F32, X + 2048)
        self.awv0 = at("s_awv0", [128, DCH * 128], BF16, Bo + 28992)
        self.sumrow = at("s_sumrow", [1, 512], F32, X + 6144)
        self.sumrec = at("s_sumrec", [1, 512], F32, X + 6400)
        self.bcast = at("s_bcast", [64, 512], F32, X + 8192)
        self.s_out0 = at("s_sout0", [128, 2 * 512], BF16, W + 8192)
        self.logits = at("s_logits", [128, 8], F32, X + 8192)
        self.gtmp = at("s_gtmp", [128, 8], F32, X + 8256)
        self.gsel = at("s_gsel", [128, 8], F32, X + 8320)
        self.gmask = at("s_gmask", [128, 8], F32, X + 8384)
        self.wg32 = at("s_wg32", [128, 64], F32, X + 8448)
        self.wg_hi = at("s_wg_hi", [128, 64], BF16, X + 8704)
        self.wg_lo = at("s_wg_lo", [128, 64], BF16, X + 8832)
        self.cw_all = at("s_cw_all", [128, 128], F32, X + 8960)
        self.ab_a = [at("s_ab_a0", [128, 512], BF16, X),
                     at("s_ab_a1", [128, 512], BF16, X + 10240)]
        self.ab_b = [at("s_ab_b0", [128, 512], BF16, X + 1024),
                     at("s_ab_b1", [128, 512], BF16, X + 11264)]
        self.h_sb = at("s_h_sb", [128, 4 * 512], BF16, X + 2048)
        self.comb_t = at("s_comb_t", [128, D], F32, X + 4096)
        self.comb_s = at("s_comb_s", [128, D], BF16, X + 8192)
        self.comb_g = at("s_comb_g", [128, D], BF16, X + 10240)
        self.wo1_sb = at("s_wo1", [128, 4 * 1024], BF16, X)
        self.s_out1 = at("s_sout1", [128, 2 * 512], BF16, X + 8192)
        self.gdbg = at("s_gdbg", [128, 128], F32, off["gdbg"])
        self.gdbg16 = at("s_gdbg16", [128, 128], BF16, off["gdbg16"])

    def alloc_psum(self):
        nc = self.nc
        self.pb0 = nc.alloc_psum_tensor("pb0", [128, 1024], F32)
        self.pb1 = nc.alloc_psum_tensor("pb1", [128, 1024], F32)
        self.pacc = nc.alloc_psum_tensor("pacc", [128, 1024], F32)
        self.pt16 = [nc.alloc_psum_tensor("pt16_0", [128, 1024], BF16),
                     nc.alloc_psum_tensor("pt16_1", [128, 1024], BF16)]


    def load_constants(self):
        o = self.P.op
        dY, V = self.dY, self.V
        for dst, src in ((self.ident16, self.ident16_d), (self.ident32, self.ident32_d),
                         (self.onehot, self.onehot_d), (self.normw, self.normw_d),
                         (self.ropeC, self.ropec_d), (self.ropeS, self.ropes_d)):
            o("y", lambda e, dst=dst, src=src: e.dma_start(out=dst[:], in_=src[:]), inc=dY)
        np0 = self.mask0_pat.shape[0]
        o("y", lambda e, np0=np0: e.dma_start(
            out=self.masks[:, 0:np0 * 512], in_=self.m0_d.rearrange("n p q -> p n q")), inc=dY)
        for t, v in ((self.neg30, EXP_SHIFT), (self.eps5, 1e-5), (self.eps6, 1e-6)):
            o("v", lambda e, t=t, v=v: e.memset(t[:], v), inc=V)
        o("v", lambda e: e.memset(self.ones[:], 1.0), inc=V)
        self.ev_const = (dY, dY.n)
        self.ev_constv = (V, V.n)

    def ln_tile(self, ev_x, eps, want_lo, rms):
        o = self.P.op
        st, xs, xq = self.st, self.xs, self.xq
        ev_sq = o("s", lambda e: e.activation(xq[:], xs[:], AF.Square,
                                              accum_out=st[:, 1:2]),
                  waits=[ev_x], inc=self.A)
        if not rms:
            ev_cp = o("s", lambda e: e.activation(self.xntm[:], xs[:], AF.Copy,
                                                  accum_out=st[:, 0:1]),
                      waits=[ev_x], inc=self.A)
            ev = o("v", lambda e: e.tensor_scalar(st[:, 2:3], st[:, 0:1], 1.0 / D, None,
                                                  op0=ALU.mult), waits=[ev_cp], inc=self.V)
            ev = o("v", lambda e: e.tensor_scalar(st[:, 3:4], st[:, 2:3], st[:, 2:3], None,
                                                  op0=ALU.mult), waits=[ev], inc=self.V)
            ev_var = o("v", lambda e: e.scalar_tensor_tensor(
                st[:, 4:5], st[:, 1:2], 1.0 / D, st[:, 3:4], op0=ALU.mult, op1=ALU.subtract),
                waits=[ev, ev_sq], inc=self.V)
        else:
            ev_var = o("v", lambda e: e.tensor_scalar(st[:, 4:5], st[:, 1:2], 1.0 / D, None,
                                                      op0=ALU.mult), waits=[ev_sq], inc=self.V)
        ev_sd = o("s", lambda e, eps=eps: e.activation(st[:, 5:6], st[:, 4:5], AF.Sqrt,
                                                       bias=eps[:]), waits=[ev_var], inc=self.A)
        ev_r = o("v", lambda e: e.reciprocal(st[:, 6:7], st[:, 5:6]), waits=[ev_sd], inc=self.V)
        if not rms:
            ev_nmr = o("v", lambda e: e.tensor_scalar(
                st[:, 7:8], st[:, 2:3], st[:, 6:7], -1.0, op0=ALU.mult, op1=ALU.mult),
                waits=[ev_r], inc=self.V)
            ev_hi = o("v", lambda e: e.tensor_scalar(
                self.xntm[:], xs[:], st[:, 6:7], st[:, 7:8], op0=ALU.mult, op1=ALU.add),
                waits=[ev_nmr], inc=self.V)
            if want_lo:
                ev_f = o("v", lambda e: e.tensor_scalar(
                    xq[:], xs[:], st[:, 6:7], st[:, 7:8], op0=ALU.mult, op1=ALU.add),
                    waits=[ev_nmr], inc=self.V)
                ev_lo = o("v", lambda e: e.scalar_tensor_tensor(
                    self.xlotm[:], self.xntm[:], -1.0, xq[:], op0=ALU.mult, op1=ALU.add),
                    waits=[ev_hi, ev_f], inc=self.V)
        else:
            ev_hi = o("v", lambda e: e.scalar_tensor_tensor(
                self.xntm[:], xs[:], st[:, 6:7], self.normw[:], op0=ALU.mult, op1=ALU.mult),
                waits=[ev_r, self.ev_const], inc=self.V)
            if want_lo:
                ev_f = o("v", lambda e: e.scalar_tensor_tensor(
                    xq[:], xs[:], st[:, 6:7], self.normw[:], op0=ALU.mult, op1=ALU.mult),
                    waits=[ev_r], inc=self.V)
                ev_lo = o("v", lambda e: e.scalar_tensor_tensor(
                    self.xlotm[:], self.xntm[:], -1.0, xq[:], op0=ALU.mult, op1=ALU.add),
                    waits=[ev_hi, ev_f], inc=self.V)
        out = {"hi": ev_hi}
        if want_lo:
            out["lo"] = ev_lo
            out["xs_free"] = ev_lo
        else:
            out["xs_free"] = ev_hi
        return out

    def transpose_tile(self, ev_src, src, dst, dst_col, bank, ev_bank_free):
        o = self.P.op
        pt = self.pt16[bank]
        last = None
        for ch in range(DCH):
            w = []
            if ch == 0:
                w = [ev_src, self.ev_const]
                if ev_bank_free is not None:
                    w.append(ev_bank_free)
            last = o("t", lambda e, ch=ch, pt=pt, src=src: e.transpose(
                pt[:, ch * 128:(ch + 1) * 128], src[:, ch * 128:(ch + 1) * 128],
                self.ident16[:]), waits=w, inc=self.T)
        dview = dst.rearrange("p (c n) -> p c n", c=DCH)[:, :, dst_col:dst_col + 128]
        sview = pt.rearrange("p (c n) -> p c n", c=DCH)
        ev_ev = o("v", lambda e, dview=dview, sview=sview: e.tensor_copy(dview, sview),
                  waits=[last], inc=self.V)
        return ev_ev

    def ln_transpose_full(self, src_dram, ntiles, dst, want_lo, lo_dst, rms, eps, guard=()):
        o = self.P.op
        ev_ev = None
        ev_ev2 = None
        ev_xs_free = None
        self.ln_tile_evs = []
        for tt in range(ntiles):
            w = list(guard) + [self.ev_const, self.ev_constv]
            if ev_xs_free is not None:
                w.append(ev_xs_free)
            ev_x = o("s", lambda e, tt=tt, src_dram=src_dram: e.dma_start(
                out=self.xs[:], in_=src_dram[tt * 128:(tt + 1) * 128, :]),
                waits=w, inc=self.dS)
            evs = self.ln_tile(ev_x, eps, want_lo, rms)
            ev_ev = self.transpose_tile(evs["hi"], self.xntm, dst, tt * 128, 0, ev_ev)
            self.ln_tile_evs.append(ev_ev)
            if want_lo:
                ev_ev2 = self.transpose_tile(evs["lo"], self.xlotm, lo_dst, tt * 128, 1, ev_ev2)
                ev_xs_free = (self.V, max(ev_ev[1], ev_ev2[1], evs["xs_free"][1]))
            else:
                ev_xs_free = (self.V, max(ev_ev[1], evs["xs_free"][1]))
        return ev_xs_free

    def ln_full_f32(self, src_dram, dst, guard=()):
        o = self.P.op
        ev_xs_free = None
        for tt in range(NT_OWN):
            w = list(guard) + [self.ev_constv]
            if ev_xs_free is not None:
                w.append(ev_xs_free)
            ev_x = o("s", lambda e, tt=tt, src_dram=src_dram: e.dma_start(
                out=self.xs[:], in_=src_dram[tt * 128:(tt + 1) * 128, :]),
                waits=w, inc=self.dS)
            evs = self.ln_tile(ev_x, self.eps5, False, False)
            ev_z = o("v", lambda e, tt=tt, dst=dst: e.tensor_scalar(
                dst[:, tt * 1024:(tt + 1) * 1024], self.xs[:],
                self.st[:, 6:7], self.st[:, 7:8], op0=ALU.mult, op1=ALU.add),
                waits=[evs["hi"]], inc=self.V)
            ev_xs_free = ev_z
        return ev_xs_free

    def attention(self, layer, ev_xn, ev_w_guard, ev_xn_tiles=None):
        if ev_xn_tiles is None:
            ev_xn_tiles = [ev_xn] * NT_SEQ
        o = self.P.op
        if layer == 0:
            wq_d, wqs_d, wk_d, wks_d, wv_d, wo_d = (self.l0_wq, self.l0_wqs, self.l0_wk,
                                                    self.l0_wks, self.l0_wv, self.l0_wo)
            sched, cc_in, cc_out = self.sched0, self.cc_s0_in, self.cc_s0_out
            rope = True
            bufs = self.qkv0
            wv_sb = self.awv0
            wo_sb, s_out = self.wo_sb, self.s_out0
            cchan = self.CC0
            dbg_name = "s0"
        else:
            wq_d, wqs_d, wk_d, wks_d, wv_d, wo_d = (self.l1_wq, None, self.l1_wk,
                                                    None, self.l1_wv, self.l1_wo)
            sched, cc_in, cc_out = self.sched1, self.cc_s1_in, self.cc_s1_out
            rope = False
            bufs = self.qkv1
            wv_sb = self.awk
            wo_sb, s_out = self.wo1_sb, self.s_out1
            cchan = self.CC2
            dbg_name = "s1"
        QB_, KB_, VA_, VB_, AT_ = bufs["Q"], bufs["K"], bufs["VA"], bufs["VB"], bufs["AT"]
        if layer == 1:
            wv_sb = self.awqs
        xnv = self.xn_att.rearrange("p (c n) -> p c n", c=DCH)

        ev_vones = {}
        for nm, vt in (("A", VA_), ("B", VB_)):
            ev_vones[nm] = o("v", lambda e, vt=vt: e.memset(
                vt.rearrange("p (k c) -> p k c", c=65)[:, :, 64:65], 1.0),
                waits=[ev_w_guard], inc=self.V)

        ev_ctx_last = None
        ev_aw_free = ev_w_guard
        ev_group_done = None
        vps_free = {0: None, 1: None}
        for g in range(4):
            wts = [(self.awq, wq_d), (self.awk, wk_d), (wv_sb, wv_d)]
            if rope:
                wts += [(self.awqs, wqs_d), (self.awks, wks_d)]
            first = True
            for sb, dsrc in wts:
                w = [ev_aw_free] if first else []
                o("g", lambda e, sb=sb, dsrc=dsrc, g=g: e.dma_start(
                    out=sb[:],
                    in_=dsrc.rearrange("(kk p) (gg c) -> gg p kk c", p=128, c=128)[g]),
                    waits=w, inc=self.aw)
                first = False
            ev_aw = (self.aw, self.aw.n)

            ev_dst_done = {}
            for dst, srcw, srcws, pband in ((QB_, self.awq, self.awqs, self.pb0),
                                            (KB_, self.awk, self.awks, self.pb1)):
                ev_written = None
                for tb in range(NQB):
                    wmm = [ev_aw, ev_xn_tiles[tb * 4 + 3]]
                    if ev_group_done is not None:
                        wmm.append(ev_group_done)
                    if ev_written is not None:
                        wmm.append(ev_written)
                    for kk in range(DCH):
                        o("t", lambda e, kk=kk, tb=tb, srcw=srcw, pband=pband, xnv=xnv: e.matmul(
                            pband[:, 0:512], srcw[:, kk * 128:(kk + 1) * 128],
                            xnv[:, kk, tb * 512:(tb + 1) * 512],
                            start=(kk == 0), stop=(kk == DCH - 1)),
                            waits=wmm if kk == 0 else [], inc=self.T)
                    if rope:
                        for kk in range(DCH):
                            o("t", lambda e, kk=kk, tb=tb, srcws=srcws, pband=pband, xnv=xnv: e.matmul(
                                pband[:, 512:1024], srcws[:, kk * 128:(kk + 1) * 128],
                                xnv[:, kk, tb * 512:(tb + 1) * 512],
                                start=(kk == 0), stop=(kk == DCH - 1)), inc=self.T)
                        ev_mm = self.tev()
                        ev1 = o("v", lambda e, tb=tb, pband=pband: e.tensor_mul(
                            self.rope_t1[:], pband[:, 512:1024],
                            self.ropeS[:, tb * 512:(tb + 1) * 512]),
                            waits=[ev_mm, self.ev_const], inc=self.V)
                        ev2 = o("v", lambda e, tb=tb, pband=pband: e.tensor_mul(
                            self.rope_t2[:], pband[:, 0:512],
                            self.ropeC[:, tb * 512:(tb + 1) * 512]), inc=self.V)
                        ev_written = o("v", lambda e, dst=dst, tb=tb: e.tensor_add(
                            dst[:, tb * 512:(tb + 1) * 512], self.rope_t1[:], self.rope_t2[:]),
                            waits=[ev2], inc=self.V)
                    else:
                        ev_mm = self.tev()
                        ev_written = o("v", lambda e, dst=dst, tb=tb, pband=pband: e.tensor_copy(
                            dst[:, tb * 512:(tb + 1) * 512], pband[:, 0:512]),
                            waits=[ev_mm], inc=self.V)
                ev_dst_done[id(dst)] = ev_written
            ev_qk_done = (self.V, max(ev_dst_done[id(QB_)][1], ev_dst_done[id(KB_)][1]))

            ev_vev = None
            for kt in range(NKT):
                wmm = [ev_aw, ev_xn_tiles[kt]]
                if ev_group_done is not None:
                    wmm.append(ev_group_done)
                if vps_free[kt % 2] is not None:
                    wmm.append(vps_free[kt % 2])
                pv = self.pacc[:, (kt % 2) * 512:(kt % 2) * 512 + 128]
                for kk in range(DCH):
                    o("t", lambda e, kk=kk, kt=kt, pv=pv, wv_sb=wv_sb, xnv=xnv: e.matmul(
                        pv, xnv[:, kk, kt * 128:(kt + 1) * 128],
                        wv_sb[:, kk * 128:(kk + 1) * 128],
                        start=(kk == 0), stop=(kk == DCH - 1)),
                        waits=wmm if kk == 0 else [], inc=self.T)
                ev_mm = self.tev()
                o("v", lambda e, kt=kt, pv=pv, VA_=VA_: e.tensor_copy(
                    VA_.rearrange("p (k c) -> p k c", c=65)[:, kt, 0:64], pv[:, 0:64]),
                    waits=[ev_mm, ev_vones["A"]], inc=self.V)
                ev_vev = o("v", lambda e, kt=kt, pv=pv, VB_=VB_: e.tensor_copy(
                    VB_.rearrange("p (k c) -> p k c", c=65)[:, kt, 0:64], pv[:, 64:128]),
                    waits=[ev_vones["B"]], inc=self.V)
                vps_free[kt % 2] = ev_vev
            ev_v_done = ev_vev
            ev_aw_free = self.tev()

            for hloc in range(2):
                hb = 64 * hloc
                vtile = VA_ if hloc == 0 else VB_
                for qb in range(NQB):
                    entries = sched[qb]
                    nent = len(entries)
                    slot_free = {0: None, 1: None}
                    at_free = {0: None, 1: None}
                    ev_av_last = None
                    ev_prev_ctx = ev_ctx_last
                    npairs = (nent + 1) // 2
                    for pi in range(npairs):
                        pair = entries[pi * 2: pi * 2 + 2]
                        slot = pi % 2
                        pst = self.pb0 if slot == 0 else self.pb1
                        at_sl = AT_[:, slot * 1024:(slot + 1) * 1024]
                        wmm = [ev_qk_done, ev_v_done, self.ev_constv]
                        if slot_free[slot] is not None:
                            wmm.append(slot_free[slot])
                        ev_masks = []
                        for j, (kt, pat) in enumerate(pair):
                            o("t", lambda e, kt=kt, j=j, qb=qb, pst=pst, hb=hb, QB_=QB_, KB_=KB_: e.matmul(
                                pst[:, j * 512:(j + 1) * 512],
                                KB_[hb:hb + 64, kt * 128:(kt + 1) * 128],
                                QB_[hb:hb + 64, qb * 512:(qb + 1) * 512],
                                start=True, stop=True),
                                waits=wmm if j == 0 else [], inc=self.T)
                            if pat is not None:
                                ev_mm = self.tev()
                                ev_mk = o("v", lambda e, j=j, pat=pat, pst=pst: e.tensor_add(
                                    pst[:, j * 512:(j + 1) * 512],
                                    pst[:, j * 512:(j + 1) * 512],
                                    self.masks[:, pat * 512:(pat + 1) * 512]),
                                    waits=[ev_mm, self.ev_const], inc=self.V)
                                ev_masks.append(ev_mk)
                        ev_mm_pair = self.tev()
                        width = 512 * len(pair)
                        wexp = [ev_mm_pair, self.ev_constv] + ev_masks
                        if at_free[slot] is not None:
                            wexp.append(at_free[slot])
                        ev_e = o("s", lambda e, at_sl=at_sl, pst=pst, width=width: e.activation(
                            at_sl[:, 0:width], pst[:, 0:width], AF.Exp,
                            bias=self.neg30[:], scale=0.125), waits=wexp, inc=self.A)
                        slot_free[slot] = ev_e
                        for j, (kt, pat) in enumerate(pair):
                            first = (pi == 0 and j == 0)
                            last = (pi * 2 + j == nent - 1)
                            wav = [ev_e]
                            if first and ev_prev_ctx is not None:
                                wav.append(ev_prev_ctx)
                            o("t", lambda e, kt=kt, j=j, at_sl=at_sl, vtile=vtile,
                                first=first, last=last: e.matmul(
                                self.pacc[0:65, 0:512],
                                vtile[:, kt * 65: kt * 65 + 65],
                                at_sl[:, j * 512:(j + 1) * 512],
                                start=first, stop=last),
                                waits=wav if j == 0 else [], inc=self.T)
                        at_free[slot] = self.tev()
                        ev_av_last = self.tev()
                    ev_sr = o("v", lambda e: e.tensor_copy(self.sumrow[:], self.pacc[64:65, 0:512]),
                              waits=[ev_av_last], inc=self.V)
                    ev_rc = o("v", lambda e: e.reciprocal(self.sumrec[:], self.sumrow[:]),
                              waits=[ev_sr], inc=self.V)
                    ev_bc = o("t", lambda e: e.matmul(
                        self.pacc[0:64, 512:1024], self.ones[:], self.sumrec[:],
                        start=True, stop=True), waits=[ev_rc, self.ev_constv], inc=self.T)
                    ev_bs = o("v", lambda e: e.tensor_copy(self.bcast[:], self.pacc[0:64, 512:1024]),
                              waits=[ev_bc], inc=self.V)
                    ev_cx = o("v", lambda e, g=g, hb=hb, qb=qb: e.tensor_mul(
                        self.ctx[hb:hb + 64, g * S + qb * 512: g * S + (qb + 1) * 512],
                        self.pacc[0:64, 0:512], self.bcast[:]),
                        waits=[ev_bs], inc=self.V)
                    ev_ctx_last = ev_cx
            ev_group_done = self.vev()
        self.ev_attn_done = ev_ctx_last
        if layer == 0 and "ctx0" in self.dbg_d:
            o("y", lambda e: e.dma_start(out=self.dbg_d["ctx0"][:], in_=self.ctx[:]),
              waits=[ev_ctx_last], inc=self.dY)

        ev_wo = o("g", lambda e, wo_sb=wo_sb, wo_d=wo_d: e.dma_start(
            out=wo_sb[:], in_=wo_d.rearrange("(cc p) o -> p cc o", p=128)),
            waits=[self.ev_attn_done], inc=self.aw)
        ctxv = self.ctx.rearrange("p (c n) -> p c n", c=4)
        ev_store_last = None
        ps_free = {0: None, 1: None}
        sl_free = {0: None, 1: None}
        for oh in range(2):
            for tt in range(NT_SEQ):
                sl = (tt * 2 + oh) % 2
                ps = self.pacc[:, sl * 512:(sl + 1) * 512]
                wmm = [ev_wo, self.ev_attn_done]
                if ps_free[sl] is not None:
                    wmm.append(ps_free[sl])
                for cc in range(4):
                    o("t", lambda e, cc=cc, tt=tt, oh=oh, ps=ps, ctxv=ctxv, wo_sb=wo_sb: e.matmul(
                        ps, ctxv[:, cc, tt * 128:(tt + 1) * 128],
                        wo_sb[:, cc * 1024 + oh * 512: cc * 1024 + (oh + 1) * 512],
                        start=(cc == 0), stop=(cc == 3)),
                        waits=wmm if cc == 0 else [], inc=self.T)
                ev_mm = self.tev()
                wv = [ev_mm]
                if sl_free[sl] is not None:
                    wv.append(sl_free[sl])
                ev_ev = o("v", lambda e, sl=sl, ps=ps, s_out=s_out: e.tensor_copy(
                    s_out[:, sl * 512:(sl + 1) * 512], ps), waits=wv, inc=self.V)
                ps_free[sl] = ev_ev
                ev_st = o("y", lambda e, tt=tt, oh=oh, sl=sl, cc_in=cc_in, s_out=s_out: e.dma_start(
                    out=cc_in[oh][tt * 128:(tt + 1) * 128, :],
                    in_=s_out[:, sl * 512:(sl + 1) * 512]), waits=[ev_ev], inc=self.dY)
                sl_free[sl] = ev_st
                ev_store_last = ev_st
            o("g", lambda e, oh=oh, cc_in=cc_in, cc_out=cc_out: e.collective_compute(
                "ReduceScatter", ALU.add, replica_groups=PAIR_GROUPS,
                ins=[cc_in[oh][:]], outs=[cc_out[oh][:]]),
                waits=[ev_store_last], inc=cchan, drain=False)
        if dbg_name in self.dbg_d:
            for oh in range(2):
                o("y", lambda e, oh=oh, cc_in=cc_in, dbg_name=dbg_name: e.dma_start(
                    out=self.dbg_d[dbg_name].rearrange("a (h c) -> a h c", h=2)[:, oh, :],
                    in_=cc_in[oh][:]), waits=[ev_store_last], inc=self.dY)
        self.ev_cc = (cchan, cchan.n)

    def gating(self, layer, xn_hi, xn_lo, wg_d, ev_xnown):
        o = self.P.op
        ev_w = o("y", lambda e, wg_d=wg_d: e.dma_start(
            out=self.wg32[:], in_=wg_d.rearrange("(kk p) ee -> p kk ee", p=128)),
            waits=[ev_xnown], inc=self.dY)
        ev_hi = o("v", lambda e: e.tensor_copy(self.wg_hi[:], self.wg32[:]),
                  waits=[ev_w], inc=self.V)
        ev_lo = o("v", lambda e: e.scalar_tensor_tensor(
            self.wg_lo[:], self.wg_hi[:], -1.0, self.wg32[:], op0=ALU.mult, op1=ALU.add),
            waits=[ev_hi], inc=self.V)
        if layer == 0 and "wghl" in self.dbg_d:
            ev_lo = o("v", lambda e: e.tensor_copy(self.gdbg16[:, 0:64], self.wg_hi[:]),
                      waits=[ev_lo], inc=self.V)
            ev_lo = o("v", lambda e: e.tensor_copy(self.gdbg16[:, 64:128], self.wg_lo[:]),
                      waits=[ev_lo], inc=self.V)
        xnh = xn_hi.rearrange("p (c n) -> p c n", c=DCH)
        xnl = xn_lo.rearrange("p (c n) -> p c n", c=DCH)
        ev_cw = None
        ev_plog_free = None
        for tt in range(NT_OWN):
            wmm = [ev_lo, ev_xnown]
            if ev_plog_free is not None:
                wmm.append(ev_plog_free)
            nmm = 0
            for whi, xsrc in ((self.wg_hi, xnh), (self.wg_lo, xnh), (self.wg_hi, xnl)):
                for kk in range(DCH):
                    nmm += 1
                    o("t", lambda e, kk=kk, tt=tt, whi=whi, xsrc=xsrc, nmm=nmm: e.matmul(
                        self.pb0[:, 0:8], xsrc[:, kk, tt * 128:(tt + 1) * 128],
                        whi[:, kk * 8:(kk + 1) * 8],
                        start=(nmm == 1), stop=(nmm == 24)),
                        waits=wmm if nmm == 1 else [], inc=self.T)
            ev_mm = self.tev()
            L, Gt, st = self.logits, self.gtmp, self.st
            ev = o("v", lambda e, L=L: e.tensor_copy(L[:], self.pb0[:, 0:8]),
                   waits=[ev_mm], inc=self.V)
            ev_plog_free = ev
            if layer == 0 and "glog" in self.dbg_d:
                ev = o("v", lambda e, L=L, tt=tt: e.tensor_copy(
                    self.gdbg[:, tt * 8:(tt + 1) * 8], L[:]), waits=[ev], inc=self.V)
            ev_m1 = o("v", lambda e, L=L: e.reduce_max(st[:, 8:9], L[:], axis=AX.X, negate=True),
                      waits=[ev], inc=self.V)
            ev_exp = o("s", lambda e, L=L, Gt=Gt: e.activation(Gt[:], L[:], AF.Exp,
                                                               bias=st[:, 8:9]),
                       waits=[ev_m1], inc=self.A)
            if layer == 1:
                ev_z = o("v", lambda e, Gt=Gt: e.reduce_sum(st[:, 12:13], Gt[:], axis=AX.X),
                         waits=[ev_exp], inc=self.V)
            ev = o("v", lambda e, L=L: e.tensor_scalar(
                self.gsel[:], L[:], st[:, 8:9], 0.0, op0=ALU.add, op1=ALU.is_ge),
                waits=[ev_m1], inc=self.V)
            ev = o("v", lambda e, L=L: e.scalar_tensor_tensor(
                self.gmask[:], self.gsel[:], NEG_BIG, L[:], op0=ALU.mult, op1=ALU.add),
                waits=[ev], inc=self.V)
            ev = o("v", lambda e: e.reduce_max(st[:, 9:10], self.gmask[:], axis=AX.X, negate=True),
                   waits=[ev], inc=self.V)
            ev = o("v", lambda e, L=L: e.tensor_scalar(
                self.gsel[:], L[:], st[:, 9:10], 0.0, op0=ALU.add, op1=ALU.is_ge),
                waits=[ev], inc=self.V)
            ev_sel = o("v", lambda e, Gt=Gt: e.tensor_mul(Gt[:], Gt[:], self.gsel[:]),
                       waits=[ev, ev_exp], inc=self.V)
            ev_s2 = o("v", lambda e, Gt=Gt: e.reduce_sum(st[:, 10:11], Gt[:], axis=AX.X),
                      waits=[ev_sel], inc=self.V)
            if layer == 1:
                ev_s2 = o("v", lambda e: e.scalar_tensor_tensor(
                    st[:, 10:11], st[:, 12:13], 1e-9, st[:, 10:11], op0=ALU.mult, op1=ALU.add),
                    waits=[ev_s2, ev_z], inc=self.V)
            ev_rc = o("v", lambda e: e.reciprocal(st[:, 11:12], st[:, 10:11]),
                      waits=[ev_s2], inc=self.V)
            ev_cwt = o("v", lambda e, tt=tt, Gt=Gt: e.tensor_scalar(
                self.cw_all[:, tt * 8:(tt + 1) * 8], Gt[:], st[:, 11:12], None, op0=ALU.mult),
                waits=[ev_rc], inc=self.V)
            ev_tr = o("t", lambda e, tt=tt: e.transpose(
                self.pb1[0:8, 0:128], self.cw_all[:, tt * 8:(tt + 1) * 8],
                self.ident32[:]), waits=[ev_cwt, self.ev_const], inc=self.T)
            ev_cw = o("v", lambda e, tt=tt: e.tensor_copy(
                self.cw_fm[:, tt * 128:(tt + 1) * 128], self.pb1[0:8, 0:128]),
                waits=[ev_tr], inc=self.V)
        return ev_cw

    def moe_experts(self, layer, xn_hi, acc, acc_init_first, ev_cw, ev_xnown, init_guard,
                    init_guard_dma=None):
        o = self.P.op
        F = F0 if layer == 0 else F1
        nsg = F // SG
        w1d = self.l0_w1 if layer == 0 else self.l1_w1
        w3d = self.l0_w3 if layer == 0 else self.l1_w3
        w2d = self.l0_w2 if layer == 0 else self.l1_w2
        xnh = xn_hi.rearrange("p (c n) -> p c n", c=DCH)
        wchan = {0: self.wS0, 1: self.wS1}
        self._ig_dma = init_guard_dma
        ev_slot_use = {0: init_guard, 1: init_guard}
        ev_load = {}
        loads = [(e, sg) for e in range(NE) for sg in range(nsg)]

        def rec_load(idx):
            e, sg = loads[idx]
            slot = idx % 2
            w = [ev_slot_use[slot]] if ev_slot_use[slot] is not None else []
            if idx < 2 and self._ig_dma is not None:
                w.append(self._ig_dma)
            o("g", lambda en, e=e, sg=sg, slot=slot, w1d=w1d: en.dma_start(
                out=self.w1_sb[slot][:],
                in_=w1d.rearrange("E (kk p) (sgn c) -> E p kk sgn c", p=128, c=SG)[e][:, :, sg]),
                waits=w, inc=wchan[slot])
            o("g", lambda en, e=e, sg=sg, slot=slot, w3d=w3d: en.dma_start(
                out=self.w3_sb[slot][:],
                in_=w3d.rearrange("E (kk p) (sgn c) -> E p kk sgn c", p=128, c=SG)[e][:, :, sg]),
                inc=wchan[slot])
            o("g", lambda en, e=e, sg=sg, slot=slot, w2d=w2d: en.dma_start(
                out=self.w2_sb[slot][:],
                in_=w2d.rearrange("E (sgn hh p) o -> E p sgn hh o", p=128, hh=4)[e][:, sg]),
                inc=wchan[slot])
            ev_load[idx] = (wchan[slot], wchan[slot].n)

        rec_load(0)
        if len(loads) > 1:
            rec_load(1)

        ev_cwb = None
        ev_cwb_war = None
        ev_ab_free = {0: None, 1: None}
        h_free = None
        pacc_free = {0: None, 1: None}
        ev_acc_tail = None
        for idx, (e, sg) in enumerate(loads):
            slot = idx % 2
            for tb in range(4):
                wcb = [ev_cw, self.ev_constv]
                if ev_cwb_war is not None:
                    wcb.append(ev_cwb_war)
                ev_mm = o("t", lambda en, e=e, tb=tb: en.matmul(
                    self.pb1[:, 0:512], self.onehot[:, e * 128:(e + 1) * 128],
                    self.cw_fm[:, tb * 512:(tb + 1) * 512], start=True, stop=True),
                    waits=wcb, inc=self.T)
                ev_cwb = o("v", lambda en: en.tensor_copy(self.cwb[:], self.pb1[:, 0:512]),
                           waits=[ev_mm], inc=self.V)
                for hh in range(4):
                    abs_ = self.ab_a[hh % 2]
                    bbs = self.ab_b[hh % 2]
                    pband = self.pb0 if hh % 2 == 0 else self.pb1
                    wmm = [ev_load[idx], ev_xnown, ev_cw]
                    if hh % 2 == 1:
                        wmm.append(ev_cwb)
                    if ev_ab_free[hh % 2] is not None:
                        wmm.append(ev_ab_free[hh % 2])
                    for kk in range(DCH):
                        o("t", lambda en, kk=kk, hh=hh, tb=tb, slot=slot, xnh=xnh,
                            pband=pband: en.matmul(
                            pband[:, 0:512],
                            self.w1_sb[slot][:, kk * SG + hh * 128: kk * SG + (hh + 1) * 128],
                            xnh[:, kk, tb * 512:(tb + 1) * 512],
                            start=(kk == 0), stop=(kk == DCH - 1)),
                            waits=wmm if kk == 0 else [], inc=self.T)
                    for kk in range(DCH):
                        o("t", lambda en, kk=kk, hh=hh, tb=tb, slot=slot, xnh=xnh,
                            pband=pband: en.matmul(
                            pband[:, 512:1024],
                            self.w3_sb[slot][:, kk * SG + hh * 128: kk * SG + (hh + 1) * 128],
                            xnh[:, kk, tb * 512:(tb + 1) * 512],
                            start=(kk == 0), stop=(kk == DCH - 1)), inc=self.T)
                    ev_mm = self.tev()
                    ev_sil = o("s", lambda en, abs_=abs_, pband=pband: en.activation(
                        abs_[:], pband[:, 0:512], AF.Silu), waits=[ev_mm], inc=self.A)
                    ev_bs = o("v", lambda en, bbs=bbs, pband=pband: en.tensor_mul(
                        bbs[:], pband[:, 512:1024], self.cwb[:]),
                        waits=[ev_mm, ev_cwb], inc=self.V)
                    wh = [ev_sil, ev_bs]
                    if h_free is not None and hh < 2:
                        wh.append(h_free)
                    ev_h = o("v", lambda en, hh=hh, abs_=abs_, bbs=bbs: en.tensor_mul(
                        self.h_sb[:, hh * 512:(hh + 1) * 512], abs_[:], bbs[:]),
                        waits=wh, inc=self.V)
                    ev_ab_free[hh % 2] = ev_h
                    ev_cwb_war = ev_h
                ev_h_all = ev_h
                if idx == 0 and tb == 0 and "h00" in self.dbg_d and layer == 0:
                    o("y", lambda en: en.dma_start(
                        out=self.dbg_d["h00"][:], in_=self.h_sb[:]),
                        waits=[ev_h_all], inc=self.dY)
                    o("y", lambda en: en.dma_start(
                        out=self.dbg_d["cwb0"][:, 0:512], in_=self.cwb[:]),
                        waits=[ev_cwb], inc=self.dY)
                for ttl in range(4):
                    for oh in range(2):
                        psl = (ttl * 2 + oh) % 2
                        ps = self.pacc[:, psl * 512:(psl + 1) * 512]
                        wmm = [ev_h_all, ev_load[idx]]
                        if pacc_free[psl] is not None:
                            wmm.append(pacc_free[psl])
                        for hh in range(4):
                            o("t", lambda en, hh=hh, ttl=ttl, oh=oh, slot=slot, ps=ps: en.matmul(
                                ps, self.h_sb[:, hh * 512 + ttl * 128: hh * 512 + (ttl + 1) * 128],
                                self.w2_sb[slot][:, hh * 1024 + oh * 512: hh * 1024 + (oh + 1) * 512],
                                start=(hh == 0), stop=(hh == 3)),
                                waits=wmm if hh == 0 else [], inc=self.T)
                        ev_mm = self.tev()
                        col = (tb * 4 + ttl) * 1024 + oh * 512
                        if acc_init_first and idx == 0:
                            ev_acc = o("v", lambda en, col=col, ps=ps, acc=acc: en.tensor_copy(
                                acc[:, col:col + 512], ps), waits=[ev_mm], inc=self.V)
                        else:
                            ev_acc = o("v", lambda en, col=col, ps=ps, acc=acc: en.tensor_add(
                                acc[:, col:col + 512], acc[:, col:col + 512], ps),
                                waits=[ev_mm], inc=self.V)
                        pacc_free[psl] = ev_acc
                        ev_acc_tail = ev_acc
                h_free = self.tev()
            ev_slot_use[slot] = self.tev()
            if idx + 2 < len(loads):
                rec_load(idx + 2)
        return ev_acc_tail

    def combine(self, x_src, s_src, ev_s_ready, g_src_tile_fn, ev_g_ready, dst, dchan,
                guard=None):
        o = self.P.op
        ev_last = None
        ev_xs_free = None
        for tt in range(NT_OWN):
            w = [guard] if guard is not None else []
            if ev_xs_free is not None:
                w.append(ev_xs_free)
            ev_x = o("s", lambda e, tt=tt, x_src=x_src: e.dma_start(
                out=self.xs[:], in_=x_src[tt * 128:(tt + 1) * 128, :]), waits=w, inc=self.dS)
            ev_s = None
            for oh in range(2):
                ev_s = o("y", lambda e, tt=tt, oh=oh, s_src=s_src: e.dma_start(
                    out=self.comb_s[:, oh * 512:(oh + 1) * 512],
                    in_=s_src[oh][tt * 128:(tt + 1) * 128, :]),
                    waits=[ev_s_ready] + w, inc=self.dY)
            ev_g = g_src_tile_fn(tt, w)
            ev_t = o("v", lambda e: e.tensor_mul(self.comb_t[:], self.comb_g[:], self.comb_s[:]),
                     waits=[ev_g, ev_s], inc=self.V)
            ev_o = o("v", lambda e: e.tensor_add(self.xs[:], self.xs[:], self.comb_t[:]),
                     waits=[ev_t, ev_x], inc=self.V)
            eng = "s" if dchan is self.dOUT else "y"
            ev_st = o(eng, lambda e, tt=tt, dst=dst: e.dma_start(
                out=dst[tt * 128:(tt + 1) * 128, :], in_=self.xs[:]), waits=[ev_o], inc=dchan)
            ev_xs_free = ev_st
            ev_last = ev_st
        return ev_last

    def finish(self):
        o = self.P.op
        waits = [(c, c.n) for c in (self.dS, self.dY, self.aw, self.wS0, self.wS1,
                                    self.dOUT, self.CC0, self.CC1, self.CC2)
                 if c.n > 0]
        o("y", lambda e: e.nop(), waits=waits)
        self.P.emit(self.nc)
        return self.nc

    def build(self):
        o = self.P.op
        self.load_constants()

        self.ln_transpose_full(self.x_full, NT_SEQ, self.xn_att, False, None,
                               rms=False, eps=self.eps5)
        ev_xn_att = self.vev()
        if "xn0" in self.dbg_d:
            o("y", lambda e: e.dma_start(out=self.dbg_d["xn0"][:], in_=self.xn_att[:]),
              waits=[ev_xn_att], inc=self.dY)
        if self.upto == "p1":
            return self.finish()

        self.attention(0, ev_xn_att, ev_w_guard=self.ev_const,
                       ev_xn_tiles=list(self.ln_tile_evs))
        ev_cc0 = self.ev_cc
        if self.upto == "attn0":
            return self.finish()

        self.ln_transpose_full(self.x_own, NT_OWN, self.xn_own, True, self.xn_lo,
                               rms=False, eps=self.eps5,
                               guard=[self.ev_attn_done])
        ev_xnown = self.vev()
        if "xnown" in self.dbg_d:
            o("y", lambda e: e.dma_start(out=self.dbg_d["xnown"][:], in_=self.xn_own[:]),
              waits=[ev_xnown], inc=self.dY)
        if "xnlo" in self.dbg_d:
            o("y", lambda e: e.dma_start(out=self.dbg_d["xnlo"][:], in_=self.xn_lo[:]),
              waits=[ev_xnown], inc=self.dY)
        ev_cw = self.gating(0, self.xn_own, self.xn_lo, self.l0_wg, ev_xnown)
        if "glog" in self.dbg_d:
            o("y", lambda e: e.dma_start(out=self.dbg_d["glog"][:], in_=self.gdbg[:]),
              waits=[ev_cw], inc=self.dY)
        if "wghl" in self.dbg_d:
            o("y", lambda e: e.dma_start(out=self.dbg_d["wghl"][:], in_=self.gdbg16[:]),
              waits=[ev_cw], inc=self.dY)
        if "cwall" in self.dbg_d:
            o("y", lambda e: e.dma_start(out=self.dbg_d["cwall"][:], in_=self.cw_all[:]),
              waits=[ev_cw], inc=self.dY)
        if "cwfm" in self.dbg_d:
            o("y", lambda e: e.dma_start(out=self.dbg_d["cwfm"][:], in_=self.cw_fm[:]),
              waits=[ev_cw], inc=self.dY)
        ev_moe = self.moe_experts(0, self.xn_own, self.acc, True, ev_cw, ev_xnown,
                                  init_guard=self.tev(),
                                  init_guard_dma=(self.dY, self.dY.n))
        if "acc0" in self.dbg_d:
            o("y", lambda e: e.dma_start(out=self.dbg_d["acc0"][:], in_=self.acc[:]),
              waits=[ev_moe], inc=self.dY)
        if self.upto == "moe0":
            return self.finish()

        def g_tile_l0(tt, w):
            return o("s", lambda e, tt=tt: e.activation(
                self.comb_g[:], self.acc[:, tt * 1024:(tt + 1) * 1024], AF.Silu),
                waits=[ev_moe] + w, inc=self.A)
        ev_x1 = self.combine(self.x_own, self.cc_s0_out, ev_cc0, g_tile_l0, ev_moe,
                             self.cc_x1_in, self.dY, guard=ev_moe)
        if "x1" in self.dbg_d:
            ev_x1 = o("y", lambda e: e.dma_start(
                out=self.dbg_d["x1"][:], in_=self.cc_x1_in[:]), waits=[ev_x1], inc=self.dY)
        o("g", lambda e: e.collective_compute(
            "AllGather", ALU.bypass, replica_groups=PAIR_GROUPS,
            ins=[self.cc_x1_in[:]], outs=[self.cc_x1_out[:]]), waits=[ev_x1], inc=self.CC1,
          drain=False)
        ev_cc1 = (self.CC1, self.CC1.n)
        if self.upto == "x1":
            return self.finish()

        ev_z = self.ln_full_f32(self.cc_x1_in, self.z, guard=[ev_x1])
        for loop in range(2):
            ev_ev = None
            ev_ev2 = None
            ev_xs_free = None
            for tt in range(NT_OWN):
                w = [ev_z]
                if ev_xs_free is not None:
                    w.append(ev_xs_free)
                ev_x = o("v", lambda e, tt=tt: e.tensor_copy(
                    self.xs[:], self.z[:, tt * 1024:(tt + 1) * 1024]), waits=w, inc=self.V)
                evs = self.ln_tile(ev_x, self.eps6, True, rms=True)
                ev_ev = self.transpose_tile(evs["hi"], self.xntm, self.zn_hi, tt * 128, 0, ev_ev)
                ev_ev2 = self.transpose_tile(evs["lo"], self.xlotm, self.xn_lo, tt * 128, 1, ev_ev2)
                ev_xs_free = (self.V, max(ev_ev[1], ev_ev2[1], evs["xs_free"][1]))
            ev_xnown = self.vev()
            ev_cw = self.gating(1, self.zn_hi, self.xn_lo, self.l1_rt, ev_xnown)
            ev_z = self.moe_experts(1, self.zn_hi, self.z, False, ev_cw, ev_xnown,
                                    init_guard=self.tev())
        if "z" in self.dbg_d:
            o("y", lambda e: e.dma_start(out=self.dbg_d["z"][:], in_=self.z[:]),
              waits=[ev_z], inc=self.dY)

        ev_g1 = None
        for tt in range(NT_OWN):
            ev_g1 = o("s", lambda e, tt=tt: e.activation(
                self.g1[:, tt * 1024:(tt + 1) * 1024],
                self.z[:, tt * 1024:(tt + 1) * 1024], AF.Silu), waits=[ev_z], inc=self.A)
        if "g1" in self.dbg_d:
            o("y", lambda e: e.dma_start(out=self.dbg_d["g1"][:], in_=self.g1[:]),
              waits=[ev_g1], inc=self.dY)
        if self.upto == "ffn1":
            return self.finish()

        np1 = self.mask1_pat.shape[0]
        ev_mask1 = o("y", lambda e, np1=np1: e.dma_start(
            out=self.masks[:, 0:np1 * 512], in_=self.m1_d.rearrange("n p q -> p n q")),
            waits=[self.ev_attn_done, (self.T, self.T.n), (self.V, self.V.n)], inc=self.dY)
        self.ev_const = (self.dY, self.dY.n)
        self.ln_transpose_full(self.cc_x1_out, NT_SEQ, self.xn_att, False, None,
                               rms=False, eps=self.eps5, guard=[ev_g1, ev_cc1])
        ev_xn1 = self.vev()
        if "xn1" in self.dbg_d:
            o("y", lambda e: e.dma_start(out=self.dbg_d["xn1"][:], in_=self.xn_att[:]),
              waits=[ev_xn1], inc=self.dY)
        self.attention(1, ev_xn1, ev_w_guard=self.tev(),
                       ev_xn_tiles=list(self.ln_tile_evs))
        ev_cc2 = self.ev_cc

        def g_tile_l1(tt, w):
            return o("v", lambda e, tt=tt: e.tensor_copy(
                self.comb_g[:], self.g1[:, tt * 1024:(tt + 1) * 1024]),
                waits=[ev_g1] + w, inc=self.V)
        self.combine(self.cc_x1_in, self.cc_s1_out, ev_cc2, g_tile_l1, ev_g1,
                     self.out_own, self.dOUT, guard=ev_cc2)
        o("y", lambda e: e.nop(), waits=[(self.dOUT, self.dOUT.n)])

        self.P.emit(self.nc)
        return self.nc


_BUILD_CACHE = {}


def get_program(dbg=(), upto="all"):
    key = (tuple(sorted(dbg)), upto)
    if key not in _BUILD_CACHE:
        b = Builder(dbg=dbg, upto=upto)
        nc = b.build()
        _BUILD_CACHE[key] = (nc, b)
    return _BUILD_CACHE[key]


def prep_inputs(inp):
    f = lambda a: np.ascontiguousarray(np.asarray(a), dtype=np.float32)
    x = f(inp["L0_attn_x"])
    Wq, Wk, Wv, Wo = (f(inp["L0_attn_Wq"]), f(inp["L0_attn_Wk"]),
                      f(inp["L0_attn_Wv"]), f(inp["L0_attn_Wo"]))
    wg = f(inp["L0_ffn_wg"])
    w1, w2, w3 = f(inp["L0_ffn_w1"]), f(inp["L0_ffn_w2"]), f(inp["L0_ffn_w3"])
    w_attn, w_proj = f(inp["L1_attn_w_attn"]), f(inp["L1_attn_w_proj"])
    norm_w = f(inp["L1_ffn_norm_w"])
    router = f(inp["L1_ffn_router_w"])
    v1, v3, v2 = f(inp["L1_ffn_w1"]), f(inp["L1_ffn_w3"]), f(inp["L1_ffn_w2"])

    ct, st = gen_rope_tables()
    m0, _ = gen_schedule(l0_mask_bool())
    m1, _ = gen_schedule(l1_mask_bool())
    ident16 = np.eye(128, dtype=BF)
    ident32 = np.eye(128, dtype=np.float32)
    onehot = np.zeros((8, 8 * 128), dtype=np.float32)
    for e in range(8):
        onehot[e, e * 128:(e + 1) * 128] = 1.0
    normw_b = np.ascontiguousarray(np.broadcast_to(norm_w[None, :], (128, D)))

    swap_idx = np.concatenate([np.arange(64).reshape(2, 32)[::-1].ravel() + j * 64
                               for j in range(NH)])
    WqT, WkT, WvT, WoT = Wq.T, Wk.T, Wv.T, Wo.T
    WqTs = WqT[:, swap_idx]
    WkTs = WkT[:, swap_idx]
    w_attn_T = w_attn.T
    w_projT = w_proj.T

    shared = {
        "ident16": ident16, "ident32": ident32,
        "rope_cos": ct, "rope_sin": st,
        "l0_masks": m0, "l1_masks": m1,
        "onehot": onehot, "l1_normw_bcast": normw_b,
        "l0_wg_t": np.ascontiguousarray(wg.T),
        "l0_w1_t": np.ascontiguousarray(w1.transpose(0, 2, 1)),
        "l0_w3_t": np.ascontiguousarray(w3.transpose(0, 2, 1)),
        "l0_w2_t": np.ascontiguousarray(w2.transpose(0, 2, 1)),
        "l1_router_t": np.ascontiguousarray(router),
        "l1_w1": v1, "l1_w3": v3, "l1_w2": v2,
    }
    maps = []
    for core in range(8):
        b_, r = core // 2, core % 2
        cols = slice(r * 512, (r + 1) * 512)
        m = dict(shared)
        m["x_full"] = np.ascontiguousarray(x[b_])
        m["x_own"] = np.ascontiguousarray(x[b_, r * TOK:(r + 1) * TOK])
        m["l0_wq_t"] = np.ascontiguousarray(WqT[:, cols])
        m["l0_wq_swap_t"] = np.ascontiguousarray(WqTs[:, cols])
        m["l0_wk_t"] = np.ascontiguousarray(WkT[:, cols])
        m["l0_wk_swap_t"] = np.ascontiguousarray(WkTs[:, cols])
        m["l0_wv_t"] = np.ascontiguousarray(WvT[:, cols])
        m["l0_wo_my"] = np.ascontiguousarray(WoT[cols, :])
        m["l1_wq_t"] = np.ascontiguousarray(w_attn_T[:, 0 + r * 512: 0 + (r + 1) * 512])
        m["l1_wk_t"] = np.ascontiguousarray(w_attn_T[:, 1024 + r * 512: 1024 + (r + 1) * 512])
        m["l1_wv_t"] = np.ascontiguousarray(w_attn_T[:, 2048 + r * 512: 2048 + (r + 1) * 512])
        m["l1_wo_my"] = np.ascontiguousarray(w_projT[cols, :])
        maps.append(m)
    return maps


def kernel(**inputs):
    nc, _b = get_program()
    maps = prep_inputs(inputs)
    res = run_bass_kernel_spmd(nc, maps, core_ids=list(range(8)))
    out = np.empty((B, S, D), dtype=np.float32)
    for core in range(8):
        b_, r = core // 2, core % 2
        out[b_, r * TOK:(r + 1) * TOK] = np.asarray(res.results[core]["out_own"])
    return out
